# version 13
# baseline (speedup 1.0000x reference)
import math
from contextlib import ExitStack

import numpy as np

import concourse.bass as bass
import concourse.mybir as mybir
from concourse.bass_utils import run_bass_kernel_spmd

F32 = mybir.dt.float32
I32 = mybir.dt.int32
BF16 = mybir.dt.bfloat16
AF = mybir.ActivationFunctionType
OP = mybir.AluOpType
AX = mybir.AxisListType

INV2PI = 1.0 / (2.0 * math.pi)
ROPE_BASE = 10000.0
LN_EPS = 1e-5
QEPS = 1e-5


def real_cfg():
    return dict(D=2048, DFF=8192, HD=128, NH=16, NKV=4, NB=32, SLOTS=8,
                SLENS=(4, 8, 12, 16, 20, 24, 28, 32))


def small_cfg():
    return dict(D=512, DFF=1024, HD=128, NH=4, NKV=2, NB=8, SLOTS=2,
                SLENS=(4, 8))


class Sem:
    __slots__ = ("h", "n")

    def __init__(self, nc, st, name):
        self.h = st.enter_context(nc.semaphore(name))
        self.n = 0

    def inc(self, instr, k=1):
        instr.then_inc(self.h, k)
        self.n += k
        return self.n


hq_dma, hq_q, hq_sg, hq_bf, hq_tr_mark = {}, {}, {}, {}, {}
wd_dma, wd_cmp, wd_add, wd_tr = {}, {}, {}, {}
dn_mm, dn_ev = {}, {}


def _dbuf(stack, nc, name, cols, dtype, n=2):
    t = stack.enter_context(nc.sbuf_tensor(name, [128, n * cols], dtype))
    return lambda p: t[:, (p % n) * cols:((p % n) + 1) * cols]


def build_nc(cfg, upto=99, dbg=()):
    D, DFF, HD = cfg["D"], cfg["DFF"], cfg["HD"]
    NH, NKV, NB = cfg["NH"], cfg["NKV"], cfg["NB"]
    SLOTS, SLENS = cfg["SLOTS"], cfg["SLENS"]
    KV = NKV * HD
    DC = D // 128
    HJ = HD // 2
    NTOK = SLOTS * 128
    S = NB * 128
    NOC = D // 512
    HPO = 512 // HD
    FFC = DFF // 512
    NGR = DFF // 1024
    NUNITS = SLOTS * NH
    REP = NH // NKV
    scale_q = 1.0 / math.sqrt(HD)
    maxL = SLENS[-1] * 128
    assert NH == DC

    for dd in (hq_dma, hq_q, hq_sg, hq_bf, hq_tr_mark, wd_dma, wd_cmp,
               wd_add, wd_tr, dn_mm, dn_ev):
        dd.clear()
    nc = bass.Bass("TRN2", target_bir_lowering=False)
    st = ExitStack()

    x_q = nc.dram_tensor("x_q", [NTOK, D], F32, kind="ExternalInput")
    x_kv = nc.dram_tensor("x_kv", [S, D], F32, kind="ExternalInput")
    wq_d = nc.dram_tensor("wq", [D, D], F32, kind="ExternalInput")
    bq_d = nc.dram_tensor("bq", [1, D], F32, kind="ExternalInput")
    wk_d = nc.dram_tensor("wk", [KV, D], F32, kind="ExternalInput")
    bk_d = nc.dram_tensor("bk", [1, KV], F32, kind="ExternalInput")
    wv_d = nc.dram_tensor("wv", [KV, D], F32, kind="ExternalInput")
    bv_d = nc.dram_tensor("bv", [1, KV], F32, kind="ExternalInput")
    wo_d = nc.dram_tensor("wo", [D, D], F32, kind="ExternalInput")
    bo_d = nc.dram_tensor("bo", [1, D], F32, kind="ExternalInput")
    wg_d = nc.dram_tensor("wg", [DFF, D], F32, kind="ExternalInput")
    wu_d = nc.dram_tensor("wu", [DFF, D], F32, kind="ExternalInput")
    wd_d = nc.dram_tensor("wd", [D, DFF], F32, kind="ExternalInput")
    qpos_d = nc.dram_tensor("qpos", [128, SLOTS], F32, kind="ExternalInput")
    thr_d = nc.dram_tensor("thr", [128, 1], F32, kind="ExternalInput")
    out_d = nc.dram_tensor("out", [NTOK, D], F32, kind="ExternalOutput")
    hscr = nc.dram_tensor("hscr", [NTOK, DFF], F32, kind="Internal")
    x1scr = nc.dram_tensor("x1scr", [NTOK, D], F32, kind="Internal")
    x1raw = nc.dram_tensor("x1raw", [NTOK, D], F32, kind="Internal")
    xkv_bf = nc.dram_tensor("xkv_bf", [S, D], BF16, kind="Internal")
    xq_bf = nc.dram_tensor("xq_bf", [NTOK, D], BF16, kind="Internal")
    wq_bf = nc.dram_tensor("wq_bf", [D, D], BF16, kind="Internal")
    wk_bf = nc.dram_tensor("wk_bf", [KV, D], BF16, kind="Internal")
    wv_bf = nc.dram_tensor("wv_bf", [KV, D], BF16, kind="Internal")
    wo_bf = nc.dram_tensor("wo_bf", [D, D], BF16, kind="Internal")
    wg_q = nc.dram_tensor("wg_q", [DFF, D], BF16, kind="Internal")
    wu_q = nc.dram_tensor("wu_q", [DFF, D], BF16, kind="Internal")
    wd_q = nc.dram_tensor("wd_q", [D, DFF], BF16, kind="Internal")
    dbg_d = {}
    for nm, shp in dbg:
        dbg_d[nm] = nc.dram_tensor("dbg_" + nm, list(shp), F32, kind="ExternalOutput")

    names = ["dSy", "dSc", "dCa", "dOut", "tXT", "tMM", "tPT", "tAV", "tMS",
             "vCh", "vMi", "vRp", "sCh", "sEv", "sEx", "gCh", "dBias", "dIn",
             "dDbg", "cXK", "cXQ", "cWKV", "cWQ", "cWO", "dTW"]
    SM = {n: Sem(nc, st, n) for n in names}

    def spair(nm):
        return [Sem(nc, st, nm + "0"), Sem(nc, st, nm + "1")]

    SP = {nm: spair(nm) for nm in
          ("xtg", "wst", "wst4", "xq32", "hwr", "hst6",
           "wd6", "x1sp", "x1rd", "outw", "x1rw", "x1fr")}
    SP["mst"] = [Sem(nc, st, "mst%d" % q) for q in range(4)]
    SP["w5st"] = [Sem(nc, st, "w5q%d" % q) for q in range(6)]
    SP["tf"] = [Sem(nc, st, "tf%d" % q) for q in range(4)]
    SP["tfw"] = [Sem(nc, st, "tfw%d" % q) for q in range(2)]

    def dump(name, ap_fn):
        if name not in dbg_d:
            return
        with nc.Block() as B:
            @B.gpsimd
            def _(g):
                SM["dDbg"].inc(g.dma_start(dbg_d[name].ap(), ap_fn()), 16)
                g.wait_ge(SM["dDbg"].h, SM["dDbg"].n)

    ident = st.enter_context(nc.sbuf_tensor("ident", [128, 128], BF16))
    pcol = st.enter_context(nc.sbuf_tensor("pcol", [128, 1], F32))
    invf = st.enter_context(nc.sbuf_tensor("invf", [128, HJ], F32))
    maskw = st.enter_context(nc.sbuf_tensor("maskw", [128, 512], BF16))
    thr_s = st.enter_context(nc.sbuf_tensor("thr_s", [128, 1], F32))
    qpos_s = st.enter_context(nc.sbuf_tensor("qpos_s", [128, SLOTS], F32))
    aux_st = ExitStack()
    ktab_c = aux_st.enter_context(nc.sbuf_tensor("ktab_c", [128, NB * HJ], BF16, side="right"))
    ktab_s = aux_st.enter_context(nc.sbuf_tensor("ktab_s", [128, NB * HJ], BF16, side="right"))
    qtab_c = aux_st.enter_context(nc.sbuf_tensor("qtab_c", [128, SLOTS * HJ], BF16, side="right"))
    qtab_s = aux_st.enter_context(nc.sbuf_tensor("qtab_s", [128, SLOTS * HJ], BF16, side="right"))
    bias_bf = st.enter_context(nc.sbuf_tensor("bias_bf", [1, 2 * D + 2 * KV], BF16))
    ones_bf = st.enter_context(nc.sbuf_tensor("ones_bf", [1, 128], BF16))
    onescol_bf = st.enter_context(nc.sbuf_tensor("onescol_bf", [128, 1], BF16))
    onescol = st.enter_context(nc.sbuf_tensor("onescol", [128, 1], F32))
    onesrow = st.enter_context(nc.sbuf_tensor("onesrow", [1, 128], F32))
    qnsq = st.enter_context(nc.sbuf_tensor("qnsq", [128, NUNITS], F32))
    negM = st.enter_context(nc.sbuf_tensor("negM", [128, NUNITS], F32))
    kml2 = st.enter_context(nc.sbuf_tensor("kml2", [128, NKV], F32))
    krow = st.enter_context(nc.sbuf_tensor("krow", [1, 16], F32))
    krow_bf = st.enter_context(nc.sbuf_tensor("krow_bf", [1, 16], BF16))
    scl = {}
    for nm in ("s_x", "r_x", "s_h", "r_h", "sgx", "sux", "shd"):
        scl[nm] = st.enter_context(nc.sbuf_tensor(nm, [128, SLOTS], F32))
    wsc = st.enter_context(nc.sbuf_tensor("wsc", [128, 16], F32))
    stat = st.enter_context(nc.sbuf_tensor("stat", [128, 64], F32))
    stat2 = st.enter_context(nc.sbuf_tensor("stat2", [128, 64], F32))
    mean_tiles = []
    mt_bounds = []
    for wten_, rows_, colw_ in ((wg_d, DFF, D), (wu_d, DFF, D), (wd_d, D, DFF)):
        cnt0 = len(mean_tiles)
        for r0 in range(0, rows_, 128):
            for c0 in range(0, colw_, 1024):
                mean_tiles.append((wten_, r0, c0, min(1024, colw_ - c0)))
        mt_bounds.append(len(mean_tiles) - cnt0)
    n_mt = len(mean_tiles)
    asum = aux_st.enter_context(nc.sbuf_tensor("asum", [128, n_mt], F32, side="right"))
    mst_t = aux_st.enter_context(nc.sbuf_tensor("mstage", [128, 4 * 1024], BF16, side="right"))

    def mstage(p):
        return mst_t[:, (p % 4) * 1024:((p % 4) + 1) * 1024]

    junk_bf = st.enter_context(nc.sbuf_tensor("junk_bf", [128, max(D, 2048)], BF16))
    t11 = st.enter_context(nc.sbuf_tensor("t11", [1, 16], F32))
    posc = st.enter_context(nc.sbuf_tensor("posc", [128, 1], F32))
    tabt = aux_st.enter_context(nc.sbuf_tensor("tabt", [128, 4 * HJ], F32, side="right"))
    tabti = aux_st.enter_context(nc.sbuf_tensor("tabti", [128, HJ], I32, side="right"))
    p0_st = ExitStack()
    iot512 = p0_st.enter_context(nc.sbuf_tensor("iot512", [128, 512], F32, side="right"))


    with nc.Block() as B:
        @B.gpsimd
        def _(g):
            g.iota(iot512[:, :], [[1, 512]], channel_multiplier=0,
                   allow_small_or_imprecise_dtypes=True)
            g.iota(pcol[:, :], [[0, 1]], channel_multiplier=1,
                   allow_small_or_imprecise_dtypes=True)
            g.iota(invf[:, :], [[1, HJ]], channel_multiplier=0,
                   allow_small_or_imprecise_dtypes=True)
            SM["gCh"].inc(g.memset(ident[:, :], 0.0))
            g.wait_ge(SM["gCh"].h, SM["gCh"].n)
            g.affine_select(out=ident[:, :], in_=ident[:, :],
                            compare_op=OP.not_equal, fill=1.0,
                            base=0, pattern=[[-1, 128]], channel_multiplier=1)
            g.memset(ones_bf[:, :], 1.0)
            g.memset(onescol_bf[:, :], 1.0)
            g.memset(onescol[:, :], 1.0)
            g.memset(onesrow[:, :], 1.0)
            g.memset(stat[:, :], 0.0)
            g.memset(stat2[:, :], 0.0)
            g.memset(t11[:, :], 0.0)
            g.memset(krow[:, :], 0.0)
            g.memset(krow_bf[:, :], 0.0)
            g.memset(wsc[:, :], 0.0)
            g.memset(junk_bf[:, :], 0.0)
            g.memset(qnsq[:, :], 0.0)
            for nm in scl:
                g.memset(scl[nm][:, :], 1.0)
            for dst, src in ((bias_bf[:, 0:D], bq_d), (bias_bf[:, D:2 * D], bo_d),
                             (bias_bf[:, 2 * D:2 * D + KV], bk_d),
                             (bias_bf[:, 2 * D + KV:2 * D + 2 * KV], bv_d)):
                SM["dBias"].inc(g.dma_start(dst, src.ap()), 16)
            SM["cWKV"].inc(g.dma_start(wk_bf.ap(), wk_d.ap()), 16)
            SM["cWKV"].inc(g.dma_start(wv_bf.ap(), wv_d.ap()), 16)
            for q in range(4):
                SM["cXK"].inc(g.dma_start(
                    xkv_bf.ap()[q * (S // 4):(q + 1) * (S // 4), :],
                    x_kv.ap()[q * (S // 4):(q + 1) * (S // 4), :]), 16)
            SM["cXQ"].inc(g.dma_start(xq_bf.ap(), x_q.ap()), 16)
            for q in range(2):
                SM["cWQ"].inc(g.dma_start(
                    wq_bf.ap()[q * (D // 2):(q + 1) * (D // 2), :],
                    wq_d.ap()[q * (D // 2):(q + 1) * (D // 2), :]), 16)
                SM["cWO"].inc(g.dma_start(
                    wo_bf.ap()[q * (D // 2):(q + 1) * (D // 2), :],
                    wo_d.ap()[q * (D // 2):(q + 1) * (D // 2), :]), 16)

        @B.sync
        def _(s):
            SM["dIn"].inc(s.dma_start(thr_s[:, :], thr_d.ap()), 16)
            SM["dIn"].inc(s.dma_start(qpos_s[:, :], qpos_d.ap()), 16)

    with nc.Block() as B:
        @B.vector
        def _(v):
            v.wait_ge(SM["dIn"].h, SM["dIn"].n)
            v.tensor_scalar(maskw[:, :], iot512[:, :], thr_s[:, :], None, OP.is_lt)

        @B.scalar
        def _(sc):
            SM["sEv"].inc(sc.activation(invf[:, :], invf[:, :], AF.Exp,
                                        bias=0.0, scale=-math.log(ROPE_BASE) / HJ))

    def emit_table(B, pos_col, out_c, out_s, pre_scale, add_base):
        u = tabt[:, 0:HJ]
        f = tabt[:, HJ:2 * HJ]
        g2 = tabt[:, 2 * HJ:3 * HJ]
        arg = tabt[:, 3 * HJ:4 * HJ]

        @B.vector
        def _(v):
            if add_base is not None:
                SM["vCh"].inc(v.tensor_scalar(posc[:, :], pcol[:, :],
                                              float(add_base), None, OP.add))
                v.wait_ge(SM["vCh"].h, SM["vCh"].n)
                pc = posc[:, :]
            else:
                pc = pos_col
            SM["vCh"].inc(v.tensor_scalar(u, invf[:, :], pc, INV2PI, OP.mult, OP.mult))
            v.wait_ge(SM["vCh"].h, SM["vCh"].n)
            SM["vCh"].inc(v.tensor_copy(tabti[:, :], u))
            v.wait_ge(SM["vCh"].h, SM["vCh"].n)
            SM["vCh"].inc(v.tensor_copy(g2, tabti[:, :]))
            v.wait_ge(SM["vCh"].h, SM["vCh"].n)
            SM["vCh"].inc(v.tensor_tensor(f, u, g2, OP.subtract))
            v.wait_ge(SM["vCh"].h, SM["vCh"].n)
            SM["vCh"].inc(v.tensor_scalar(g2, f, 0.5, None, OP.is_ge))
            v.wait_ge(SM["vCh"].h, SM["vCh"].n)
            SM["vCh"].inc(v.tensor_tensor(arg, g2, f, OP.subtract))

        @B.scalar
        def _(sc):
            sc.wait_ge(SM["vCh"].h, SM["vCh"].n)
            SM["sEx"].inc(sc.activation(out_s, arg, AF.Sin,
                                        bias=0.0, scale=-2.0 * math.pi))

        @B.vector
        def _(v):
            v.wait_ge(SM["sEx"].h, SM["sEx"].n)
            SM["vCh"].inc(v.tensor_scalar(u, u, 0.25, None, OP.add))
            v.wait_ge(SM["vCh"].h, SM["vCh"].n)
            SM["vCh"].inc(v.tensor_copy(tabti[:, :], u))
            v.wait_ge(SM["vCh"].h, SM["vCh"].n)
            SM["vCh"].inc(v.tensor_copy(g2, tabti[:, :]))
            v.wait_ge(SM["vCh"].h, SM["vCh"].n)
            SM["vCh"].inc(v.tensor_tensor(f, u, g2, OP.subtract))
            v.wait_ge(SM["vCh"].h, SM["vCh"].n)
            SM["vCh"].inc(v.tensor_scalar(g2, f, 0.5, None, OP.is_ge))
            v.wait_ge(SM["vCh"].h, SM["vCh"].n)
            SM["vCh"].inc(v.tensor_tensor(arg, g2, f, OP.subtract))

        @B.scalar
        def _(sc):
            sc.wait_ge(SM["vCh"].h, SM["vCh"].n)
            SM["sEx"].inc(sc.activation(out_c, arg, AF.Sin,
                                        bias=0.0, scale=-2.0 * math.pi))

        @B.vector
        def _(v):
            v.wait_ge(SM["sEx"].h, SM["sEx"].n)
            if pre_scale != 1.0:
                v.tensor_scalar(out_s, out_s, float(pre_scale), None, OP.mult)
                SM["vCh"].inc(v.tensor_scalar(out_c, out_c, float(pre_scale),
                                              None, OP.mult))
                v.wait_ge(SM["vCh"].h, SM["vCh"].n)

    p0_st.close()

    bank_free = {}

    def emit_ct_tile(B, src_ap, width, stage_buf, tp_banks, dest_ap, war,
                     dest_war=None, stsem=None, evac="scalar"):
        nch = width // 128

        @B.gpsimd
        def _(g):
            if war is not None:
                g.wait_ge(war[0].h, war[1])
            stsem.inc(g.dma_start(stage_buf, src_ap), 16)
        cast_n = stsem.n
        grp_t = []

        @B.tensor
        def _(te):
            te.wait_ge(stsem.h, cast_n)
            for j in range(nch):
                bank = tp_banks[(j // 8) % 2]
                if j % 8 == 0:
                    te.wait_ge(SM["sEv"].h, bank_free.get(id(bank), 0))
                ins = te.transpose(bank[:, (j % 8) * 128:(j % 8 + 1) * 128],
                                   stage_buf[:, j * 128:(j + 1) * 128], ident[:, :])
                if j % 8 == 7 or j == nch - 1:
                    grp_t.append(SM["tXT"].inc(ins))

        def evac_body(e):
            if dest_war is not None:
                e.wait_ge(dest_war[0].h, dest_war[1])
            for gi in range((nch + 7) // 8):
                bank = tp_banks[gi % 2]
                e.wait_ge(SM["tXT"].h, grp_t[gi])
                lo = gi * 8
                hi = min(lo + 8, nch)
                if evac == "scalar":
                    SM["sEv"].inc(e.activation(dest_ap(lo, hi),
                                               bank[:, 0:(hi - lo) * 128],
                                               AF.Copy, bias=0.0, scale=1.0))
                else:
                    SM["sEv"].inc(e.tensor_copy(dest_ap(lo, hi),
                                                bank[:, 0:(hi - lo) * 128]))
                bank_free[id(bank)] = SM["sEv"].n
        if evac == "scalar":
            B.scalar(evac_body)
        else:
            B.vector(evac_body)
        return SM["sEv"].n

    attn_st = ExitStack()
    qf = attn_st.enter_context(nc.sbuf_tensor("qf", [128, NUNITS * 128], BF16))
    p13 = ExitStack()
    kf = p13.enter_context(nc.sbuf_tensor("kf", [128, NKV * S], BF16))
    vsb = p13.enter_context(nc.sbuf_tensor("vsb", [128, NB * KV], BF16))

    mt_p1 = {}
    mt_p2 = {}
    n_hosts = NB + NOC * SLOTS
    per_host = (n_mt + n_hosts - 1) // n_hosts
    assert per_host <= 8
    for t in range(n_mt):
        host = min(t // per_host, n_hosts - 1)
        if host < NB:
            mt_p1.setdefault(host, []).append(t)
        else:
            mt_p2.setdefault(host - NB, []).append(t)
    mt_dma, mt_red = {}, {}

    def emit_mean(B, tiles):
        for t in tiles:
            wten, r0, c0, cw = mean_tiles[t]

            @B.gpsimd
            def _(g, t=t, wten=wten, r0=r0, c0=c0, cw=cw):
                if t >= 4:
                    g.wait_ge(SM["vMi"].h, mt_red[t - 4])
                mt_dma[t] = SP["mst"][t % 4].inc(g.dma_start(
                    mstage(t)[:, 0:cw], wten.ap()[r0:r0 + 128, c0:c0 + cw]), 16)

            @B.vector
            def _(v, t=t, cw=cw):
                v.wait_ge(SP["mst"][t % 4].h, mt_dma[t])
                mt_red[t] = SM["vMi"].inc(v.reduce_sum(
                    asum[:, t:t + 1], mstage(t)[:, 0:cw], AX.X,
                    apply_absolute_value=True))

    if upto >= 1:
        p1 = ExitStack()
        wkT = p1.enter_context(nc.sbuf_tensor("wkT", [128, DC * KV], BF16))
        wvT = p1.enter_context(nc.sbuf_tensor("wvT", [128, DC * KV], BF16))
        xtg = _dbuf(p1, nc, "xtg", DC * 256, BF16)
        krope = _dbuf(p1, nc, "krope", KV, BF16)
        ropet = p1.enter_context(nc.sbuf_tensor("ropet", [128, 4 * NKV * HJ], F32))
        p1p = ExitStack()
        kps = p1p.enter_context(nc.psum_tensor("kps", [128, 1024], F32))
        vps = p1p.enter_context(nc.psum_tensor("vps", [128, 1024], F32))
        tkp = p1p.enter_context(nc.psum_tensor("tkp", [128, 2048], BF16))

        with nc.Block() as B:
            @B.sync
            def _(s):
                s.wait_ge(SM["cWKV"].h, SM["cWKV"].n)
                for wbf, wT in ((wk_bf, wkT), (wv_bf, wvT)):
                    for j in range(DC):
                        SM["dTW"].inc(s.dma_start(
                            wT[:, j * KV:(j + 1) * KV],
                            wbf.ap()[0:KV, j * 128:(j + 1) * 128],
                            transpose=True), 16)
            twv_tot = SM["dTW"].n

            blk_kmm, blk_vmm, blk_rope, blk_ktr = [], [], [], []
            blk_kf_ev, blk_v_ev = [], []
            xtg_marks = []
            for g4 in range(NB // 2):
                @B.sync
                def _(s, g4=g4):
                    if g4 == 0:
                        s.wait_ge(SM["cXK"].h, SM["cXK"].n)
                    if g4 >= 2:
                        s.wait_ge(SM["tMM"].h, blk_vmm[(g4 - 2) * 2 + 1])
                    for j in range(DC):
                        SP["xtg"][g4 % 2].inc(s.dma_start(
                            xtg(g4)[:, j * 256:(j + 1) * 256],
                            xkv_bf.ap()[g4 * 256:(g4 + 1) * 256,
                                        j * 128:(j + 1) * 128],
                            transpose=True), 16)
                xtg_marks.append(SP["xtg"][g4 % 2].n)

                for kb in range(g4 * 2, g4 * 2 + 2):
                    emit_table(B, None, ktab_c[:, kb * HJ:(kb + 1) * HJ],
                               ktab_s[:, kb * HJ:(kb + 1) * HJ], 1.0, 128 * kb)
                    emit_mean(B, mt_p1.get(kb, []))
                    half = (kb % 2) * 512
                    kslice = kps[:, half:half + KV]
                    vslice = vps[:, half:half + KV]

                    @B.tensor
                    def _(te, g4=g4, kb=kb, kslice=kslice, vslice=vslice):
                        if kb == 0:
                            te.wait_ge(SM["dBias"].h, SM["dBias"].n)
                            te.wait_ge(SM["dTW"].h, twv_tot)
                        if kb % 2 == 0:
                            te.wait_ge(SP["xtg"][g4 % 2].h, xtg_marks[g4])
                        if kb >= 2:
                            te.wait_ge(SM["vRp"].h, blk_rope[kb - 2])
                            te.wait_ge(SM["sEv"].h, blk_v_ev[kb - 2])
                        for j in range(DC):
                            lhs = xtg(g4)[:, j * 256 + (kb % 2) * 128:
                                          j * 256 + (kb % 2 + 1) * 128]
                            te.matmul(kslice, lhs, wkT[:, j * KV:(j + 1) * KV],
                                      start=(j == 0), stop=False)
                            te.matmul(vslice, lhs, wvT[:, j * KV:(j + 1) * KV],
                                      start=(j == 0), stop=False)
                        blk_kmm.append(SM["tMM"].inc(te.matmul(
                            kslice, ones_bf[0:1, :],
                            bias_bf[0:1, 2 * D:2 * D + KV],
                            start=False, stop=True)))
                        blk_vmm.append(SM["tMM"].inc(te.matmul(
                            vslice, ones_bf[0:1, :],
                            bias_bf[0:1, 2 * D + KV:2 * D + 2 * KV],
                            start=False, stop=True)))

                    @B.vector
                    def _(v, kb=kb, kslice=kslice):
                        v.wait_ge(SM["tMM"].h, blk_kmm[kb])
                        if kb >= 2:
                            v.wait_ge(SM["tPT"].h, blk_ktr[kb - 2])
                        kp = kslice.rearrange("p (h j two) -> p h j two",
                                              h=NKV, two=2)
                        ke, ko = kp[:, :, :, 0], kp[:, :, :, 1]
                        cb = ktab_c[:, kb * HJ:(kb + 1) * HJ] \
                            .rearrange("p (o j) -> p o j", o=1) \
                            .broadcast_to((128, NKV, HJ))
                        sb = ktab_s[:, kb * HJ:(kb + 1) * HJ] \
                            .rearrange("p (o j) -> p o j", o=1) \
                            .broadcast_to((128, NKV, HJ))
                        n4 = NKV * HJ

                        def rt(q):
                            return ropet[:, q * n4:(q + 1) * n4] \
                                .rearrange("p (h j) -> p h j", h=NKV)
                        v.tensor_tensor(rt(0), ke, cb, OP.mult)
                        v.tensor_tensor(rt(1), ko, sb, OP.mult)
                        v.tensor_tensor(rt(2), ko, cb, OP.mult)
                        SM["vCh"].inc(v.tensor_tensor(rt(3), ke, sb, OP.mult))
                        v.wait_ge(SM["vCh"].h, SM["vCh"].n)
                        kr = krope(kb).rearrange("p (h j two) -> p h j two",
                                                 h=NKV, two=2)
                        v.tensor_tensor(kr[:, :, :, 0], rt(0), rt(1), OP.subtract)
                        blk_rope.append(SM["vRp"].inc(v.tensor_tensor(
                            kr[:, :, :, 1], rt(2), rt(3), OP.add)))

                    @B.tensor
                    def _(te, kb=kb):
                        te.wait_ge(SM["vRp"].h, blk_rope[kb])
                        if kb >= 2:
                            te.wait_ge(SM["sEv"].h, blk_kf_ev[kb - 2])
                        toff = (kb % 2) * 1024
                        last = None
                        for h4 in range(NKV):
                            last = te.transpose(
                                tkp[:, toff + h4 * 128: toff + (h4 + 1) * 128],
                                krope(kb)[:, h4 * HD:(h4 + 1) * HD], ident[:, :])
                        blk_ktr.append(SM["tPT"].inc(last))

                    @B.scalar
                    def _(sc, kb=kb):
                        sc.wait_ge(SM["tPT"].h, blk_ktr[kb])
                        toff = (kb % 2) * 1024
                        kdst = kf[:].rearrange("p (h s) -> p h s", h=NKV) \
                            [:, :, kb * 128:(kb + 1) * 128]
                        ksrc = tkp[:, toff:toff + NKV * 128] \
                            .rearrange("p (h t) -> p h t", h=NKV)
                        blk_kf_ev.append(SM["sEv"].inc(sc.activation(
                            kdst, ksrc, AF.Copy, bias=0.0, scale=1.0)))
                        sc.wait_ge(SM["tMM"].h, blk_vmm[kb])
                        blk_v_ev.append(SM["sEv"].inc(sc.activation(
                            vsb[:, kb * KV:(kb + 1) * KV],
                            vps[:, (kb % 2) * 512:(kb % 2) * 512 + KV],
                            AF.Copy, bias=0.0, scale=1.0)))

        p1.close()
        p1b = ExitStack()
        kf2 = p1b.enter_context(nc.sbuf_tensor("kf2", [128, NKV * S], BF16))
        with nc.Block() as B:
            @B.scalar
            def _(sc):
                last = None
                for c0 in range(0, NKV * S, 2048):
                    last = sc.activation(kf2[:, c0:c0 + 2048], kf[:, c0:c0 + 2048],
                                         AF.Square, bias=0.0, scale=1.0)
                SM["sEv"].inc(last)
            sq_done = SM["sEv"].n
            red_marks = []
            for h4 in range(NKV):
                nchk = S // 512
                for p0 in range(0, nchk, 4):
                    phi = min(p0 + 4, nchk)

                    @B.tensor
                    def _(te, h4=h4, p0=p0, phi=phi):
                        te.wait_ge(SM["sEv"].h, sq_done)
                        if red_marks:
                            te.wait_ge(SM["vMi"].h, red_marks[-1])
                        last = None
                        for ch in range(p0, phi):
                            lc = ch - p0
                            bank = (kps, vps)[(lc // 2) % 2]
                            last = te.matmul(
                                bank[0:1, (lc % 2) * 512:(lc % 2) * 512 + 512],
                                onescol_bf[:, :],
                                kf2[:, h4 * S + ch * 512: h4 * S + (ch + 1) * 512],
                                start=True, stop=True)
                        SM["tMS"].inc(last)

                    @B.vector
                    def _(v, h4=h4, p0=p0, phi=phi):
                        v.wait_ge(SM["tMS"].h, SM["tMS"].n)
                        nlc = phi - p0
                        SM["vCh"].inc(v.reduce_max(t11[0:1, 8:9],
                                                   kps[0:1, 0:min(nlc, 2) * 512], AX.X))
                        v.wait_ge(SM["vCh"].h, SM["vCh"].n)
                        if nlc > 2:
                            SM["vCh"].inc(v.reduce_max(
                                t11[0:1, 9:10], vps[0:1, 0:(nlc - 2) * 512], AX.X))
                            v.wait_ge(SM["vCh"].h, SM["vCh"].n)
                            SM["vCh"].inc(v.tensor_tensor(t11[0:1, 8:9], t11[0:1, 8:9],
                                                          t11[0:1, 9:10], OP.max))
                            v.wait_ge(SM["vCh"].h, SM["vCh"].n)
                        SM["vCh"].inc(v.tensor_tensor(krow[0:1, h4:h4 + 1],
                                                      krow[0:1, h4:h4 + 1],
                                                      t11[0:1, 8:9], OP.max))
                        v.wait_ge(SM["vCh"].h, SM["vCh"].n)
                        red_marks.append(SM["vMi"].inc(v.nop()))

            @B.scalar
            def _(sc):
                sc.wait_ge(SM["vMi"].h, SM["vMi"].n)
                SM["sCh"].inc(sc.activation(krow[0:1, 8:8 + NKV], krow[0:1, 0:NKV],
                                            AF.Sqrt, bias=0.0, scale=1.0))
                sc.wait_ge(SM["sCh"].h, SM["sCh"].n)
                SM["sEv"].inc(sc.copy(krow_bf[0:1, 0:NKV], krow[0:1, 8:8 + NKV]))

            @B.tensor
            def _(te):
                te.wait_ge(SM["sEv"].h, SM["sEv"].n)
                SM["tMS"].inc(te.matmul(kps[:, 0:NKV], ones_bf[0:1, :],
                                        krow_bf[0:1, 0:NKV], start=True, stop=True))

            @B.scalar
            def _(sc):
                sc.wait_ge(SM["tMS"].h, SM["tMS"].n)
                SM["sEv"].inc(sc.activation(kml2[:, :], kps[:, 0:NKV], AF.Copy,
                                            bias=0.0, scale=1.0))
        p1b.close()
        p1p.close()
        dump("kf", lambda: kf[:, :])
        dump("vsb", lambda: vsb[:, :])
        dump("kml2", lambda: kml2[:, :])

    if upto >= 2:
        p2 = ExitStack()
        xqT = p2.enter_context(nc.sbuf_tensor("xqT", [128, SLOTS * D], BF16))
        wT2 = _dbuf(p2, nc, "wT2", DC * 512, BF16)
        qrope = _dbuf(p2, nc, "qrope", 512, BF16)
        qropet = p2.enter_context(nc.sbuf_tensor("qropet", [128, 4 * HPO * HJ], F32))
        p2p = ExitStack()
        qps = p2p.enter_context(nc.psum_tensor("qps", [128, 1024], F32))
        tq2 = p2p.enter_context(nc.psum_tensor("tq2", [128, 2048], BF16))

        with nc.Block() as B:
            @B.sync
            def _(s):
                s.wait_ge(SM["cXQ"].h, SM["cXQ"].n)
                gsz = min(512, NTOK)
                for r0 in range(0, NTOK, gsz):
                    for j in range(DC):
                        SM["dTW"].inc(s.dma_start(
                            xqT[:, j * NTOK + r0: j * NTOK + r0 + gsz],
                            xq_bf.ap()[r0:r0 + gsz, j * 128:(j + 1) * 128],
                            transpose=True), 16)
            xqt_tot = SM["dTW"].n

            q_mm, q_rope, q_tr, q_ev, q_sq = [], [], [], [], []
            wt2_marks = []
            for oc in range(NOC):
                @B.sync
                def _(s, oc=oc):
                    if oc == 0:
                        s.wait_ge(SM["cWQ"].h, SM["cWQ"].n)
                    if oc >= 2:
                        s.wait_ge(SM["tMM"].h, q_mm[-1])
                    for j in range(DC):
                        SP["wst"][oc % 2].inc(s.dma_start(
                            wT2(oc)[:, j * 512:(j + 1) * 512],
                            wq_bf.ap()[oc * 512:(oc + 1) * 512,
                                       j * 128:(j + 1) * 128],
                            transpose=True), 16)
                wt2_marks.append(SP["wst"][oc % 2].n)

                for i in range(SLOTS):
                    if oc == 0:
                        emit_table(B, qpos_s[:, i:i + 1],
                                   qtab_c[:, i * HJ:(i + 1) * HJ],
                                   qtab_s[:, i * HJ:(i + 1) * HJ], scale_q, None)
                        emit_mean(B, mt_p2.get(i, []))
                    else:
                        emit_mean(B, mt_p2.get(oc * SLOTS + i, []))
                    u0 = i * NH + oc * HPO
                    n_q = len(q_mm)

                    @B.tensor
                    def _(te, oc=oc, i=i, n_q=n_q):
                        if n_q == 0:
                            te.wait_ge(SM["dTW"].h, xqt_tot)
                        if i == 0:
                            te.wait_ge(SP["wst"][oc % 2].h, wt2_marks[oc])
                        if n_q >= 2:
                            te.wait_ge(SM["vRp"].h, q_rope[n_q - 2])
                        sl = qps[:, (n_q % 2) * 512:(n_q % 2) * 512 + 512]
                        for j in range(DC):
                            te.matmul(sl, xqT[:, j * NTOK + i * 128:
                                              j * NTOK + (i + 1) * 128],
                                      wT2(oc)[:, j * 512:(j + 1) * 512],
                                      start=(j == 0), stop=False)
                        q_mm.append(SM["tMM"].inc(te.matmul(
                            sl, ones_bf[0:1, :],
                            bias_bf[0:1, oc * 512:(oc + 1) * 512],
                            start=False, stop=True)))

                    @B.vector
                    def _(v, oc=oc, i=i, n_q=n_q):
                        v.wait_ge(SM["tMM"].h, q_mm[n_q])
                        if n_q >= 2:
                            v.wait_ge(SM["tPT"].h, q_tr[n_q - 2])
                            v.wait_ge(SM["sEx"].h, q_sq[n_q - 2])
                        sl = qps[:, (n_q % 2) * 512:(n_q % 2) * 512 + 512]
                        kp = sl.rearrange("p (h j two) -> p h j two", h=HPO, two=2)
                        ke, ko = kp[:, :, :, 0], kp[:, :, :, 1]
                        cb = qtab_c[:, i * HJ:(i + 1) * HJ] \
                            .rearrange("p (o j) -> p o j", o=1) \
                            .broadcast_to((128, HPO, HJ))
                        sb = qtab_s[:, i * HJ:(i + 1) * HJ] \
                            .rearrange("p (o j) -> p o j", o=1) \
                            .broadcast_to((128, HPO, HJ))
                        n4 = HPO * HJ

                        def rt(q):
                            return qropet[:, q * n4:(q + 1) * n4] \
                                .rearrange("p (h j) -> p h j", h=HPO)
                        v.tensor_tensor(rt(0), ke, cb, OP.mult)
                        v.tensor_tensor(rt(1), ko, sb, OP.mult)
                        v.tensor_tensor(rt(2), ko, cb, OP.mult)
                        SM["vCh"].inc(v.tensor_tensor(rt(3), ke, sb, OP.mult))
                        v.wait_ge(SM["vCh"].h, SM["vCh"].n)
                        qr = qrope(n_q).rearrange("p (h j two) -> p h j two",
                                                  h=HPO, two=2)
                        v.tensor_tensor(qr[:, :, :, 0], rt(0), rt(1), OP.subtract)
                        q_rope.append(SM["vRp"].inc(v.tensor_tensor(
                            qr[:, :, :, 1], rt(2), rt(3), OP.add)))

                    @B.scalar
                    def _(sc, u0=u0, n_q=n_q):
                        sc.wait_ge(SM["vRp"].h, q_rope[n_q])
                        last = None
                        for h4 in range(HPO):
                            jsl = (h4 + HPO * (n_q % 4)) % 16
                            last = sc.activation(
                                junk_bf[:, jsl * HD:(jsl + 1) * HD],
                                qrope(n_q)[:, h4 * HD:(h4 + 1) * HD],
                                AF.Square, bias=0.0, scale=1.0,
                                accum_out=qnsq[:, u0 + h4:u0 + h4 + 1])
                        q_sq.append(SM["sEx"].inc(last))

                    @B.tensor
                    def _(te, n_q=n_q):
                        te.wait_ge(SM["vRp"].h, q_rope[n_q])
                        if n_q >= 2:
                            te.wait_ge(SM["sEv"].h, q_ev[n_q - 2])
                        toff = (n_q % 2) * 1024
                        last = None
                        for h4 in range(HPO):
                            last = te.transpose(
                                tq2[:, toff + h4 * 128: toff + (h4 + 1) * 128],
                                qrope(n_q)[:, h4 * HD:(h4 + 1) * HD], ident[:, :])
                        q_tr.append(SM["tPT"].inc(last))

                    @B.vector
                    def _(v, u0=u0, n_q=n_q):
                        v.wait_ge(SM["tPT"].h, q_tr[n_q])
                        toff = (n_q % 2) * 1024
                        q_ev.append(SM["sEv"].inc(v.tensor_copy(
                            qf[:, u0 * 128:(u0 + HPO) * 128],
                            tq2[:, toff:toff + HPO * 128])))

        p2c = ExitStack()
        ps2c = p2c.enter_context(nc.psum_tensor("ps2c", [128, 512], F32))
        with nc.Block() as B:
            @B.vector
            def _(v):
                lo = 0
                last = None
                for wi, cnt in enumerate(mt_bounds):
                    v.wait_ge(SM["vMi"].h, mt_red[lo + cnt - 1])
                    last = v.reduce_sum(stat2[:, 8 + wi:9 + wi],
                                        asum[:, lo:lo + cnt], AX.X)
                    lo += cnt
                SM["vMi"].inc(last)

            @B.tensor
            def _(te):
                te.wait_ge(SM["vMi"].h, SM["vMi"].n)
                last = None
                for wi in range(3):
                    last = te.matmul(ps2c[0:1, wi:wi + 1], stat2[:, 8 + wi:9 + wi],
                                     onescol[:, :], start=True, stop=True)
                SM["tMS"].inc(last)

            @B.scalar
            def _(sc):
                sc.wait_ge(SM["tMS"].h, SM["tMS"].n)
                SM["sEv"].inc(sc.activation(t11[0:1, 0:3], ps2c[0:1, 0:3], AF.Copy,
                                            bias=0.0, scale=1.0))

            @B.vector
            def _(v):
                v.wait_ge(SM["sEv"].h, SM["sEv"].n)
                SM["vMi"].inc(v.tensor_scalar(t11[0:1, 4:7], t11[0:1, 0:3],
                                              1.0 / (DFF * D), QEPS, OP.mult, OP.add))

            @B.tensor
            def _(te):
                te.wait_ge(SM["vMi"].h, SM["vMi"].n)
                last = None
                for wi in range(3):
                    last = te.matmul(ps2c[:, 8 + wi:9 + wi], onesrow[0:1, :],
                                     t11[0:1, 4 + wi:5 + wi], start=True, stop=True)
                SM["tMS"].inc(last)

            @B.scalar
            def _(sc):
                sc.wait_ge(SM["tMS"].h, SM["tMS"].n)
                SM["sEv"].inc(sc.activation(wsc[:, 0:3], ps2c[:, 8:11], AF.Copy,
                                            bias=0.0, scale=1.0))

            @B.vector
            def _(v):
                v.wait_ge(SM["sEv"].h, SM["sEv"].n)
                v.tensor_scalar(wsc[:, 3:6], wsc[:, 0:3], 0.5, None, OP.mult)
                SM["vCh"].inc(v.tensor_scalar(wsc[:, 6:9], wsc[:, 0:3], -0.5,
                                              None, OP.mult))
                v.wait_ge(SM["vCh"].h, SM["vCh"].n)
        p2c.close()

        with nc.Block() as B:
            @B.scalar
            def _(sc):
                SM["sEv"].inc(sc.activation(negM[:, :], qnsq[:, :], AF.Sqrt,
                                            bias=0.0, scale=1.0))

            @B.vector
            def _(v):
                v.wait_ge(SM["sEv"].h, SM["sEv"].n)
                qn4 = negM[:, :].rearrange("p (i k r) -> p i k r", i=SLOTS, k=NKV)
                km4 = kml2[:, :].rearrange("p (a k b) -> p a k b", a=1, b=1) \
                    .broadcast_to((128, SLOTS, NKV, REP))
                SM["vCh"].inc(v.tensor_tensor(qn4, qn4, km4, OP.mult))
                v.wait_ge(SM["vCh"].h, SM["vCh"].n)
                SM["vCh"].inc(v.tensor_scalar(negM[:, :], negM[:, :], -1.02, -1.0,
                                              OP.mult, OP.add))
                v.wait_ge(SM["vCh"].h, SM["vCh"].n)
        p2p.close()
        p2.close()
        dump("qf", lambda: qf[:, :])
        dump("negM", lambda: negM[:, :])

    fac_tiles = []
    WGW = min(D, 2048)
    for rb in range(DFF // 128):
        fac_tiles.append((wg_d, wg_q, rb * 128, 0, WGW, 3))
    for rb in range(DFF // 128):
        fac_tiles.append((wu_d, wu_q, rb * 128, 0, WGW, 4))
    n_fac_gu = len(fac_tiles)
    WCW = min(2048, DFF)
    for rb in range(D // 128):
        for cc in range(DFF // WCW):
            fac_tiles.append((wd_d, wd_q, rb * 128, cc * WCW, WCW, 5))
    n_fac = len(fac_tiles)
    fac_ld, fac_mark, fac_wb = {}, {}, {}
    FAC = {}

    def fac_alloc(stack, tag, nd=4):
        FAC["wfst"] = _dbuf(stack, nc, "wfst" + tag, 2048, F32, n=nd)
        FAC["nd"] = nd
        FAC["tbuf5"] = _dbuf(stack, nc, "tbuf5" + tag, 2048, BF16, n=2)
        FAC["pnb3"] = stack.enter_context(
            nc.sbuf_tensor("pnb3" + tag, [128, 2048], BF16))

    def emit_tern(B, lo, hi):
        wfst, tbuf5, pnb3 = FAC["wfst"], FAC["tbuf5"], FAC["pnb3"]
        nd = FAC["nd"]
        for t in range(lo, hi):
            wten, wscr, r0, c0, cw, ti = fac_tiles[t]
            dec = B.sync if t % 2 == 0 else B.scalar

            @dec
            def _(eng, t=t, wten=wten, r0=r0, c0=c0, cw=cw, nd=nd):
                if t >= nd:
                    eng.wait_ge(SM["vMi"].h, fac_mark[t - nd])
                fac_ld[t] = SP["tf"][t % 4].inc(eng.dma_start(
                    wfst(t)[:, 0:cw], wten.ap()[r0:r0 + 128, c0:c0 + cw]), 16)

            @B.vector
            def _(v, t=t, cw=cw, ti=ti):
                v.wait_ge(SP["tf"][t % 4].h, fac_ld[t])
                if t >= 1:
                    v.wait_ge(SM["vMi"].h, fac_mark[t - 1])
                if t >= 2:
                    v.wait_ge(SP["tfw"][t % 2].h, fac_wb[t - 2])
                SM["vCh"].inc(v.tensor_scalar(
                    pnb3[:, 0:cw], wfst(t)[:, 0:cw],
                    wsc[:, ti + 3:ti + 4], -1.0, OP.is_le, OP.mult))
                v.wait_ge(SM["vCh"].h, SM["vCh"].n)
                fac_mark[t] = SM["vMi"].inc(v.scalar_tensor_tensor(
                    tbuf5(t)[:, 0:cw], wfst(t)[:, 0:cw], wsc[:, ti:ti + 1],
                    pnb3[:, 0:cw], OP.is_ge, OP.add))

            @B.gpsimd
            def _(g, t=t, wscr=wscr, r0=r0, c0=c0, cw=cw):
                g.wait_ge(SM["vMi"].h, fac_mark[t])
                fac_wb[t] = SP["tfw"][t % 2].inc(g.dma_start(
                    wscr.ap()[r0:r0 + 128, c0:c0 + cw], tbuf5(t)[:, 0:cw]), 16)

    if upto >= 3:
        p3 = ExitStack()
        fac_alloc(p3, "a", nd=3)
        psb = _dbuf(p3, nc, "psb", maxL, BF16)
        pnb = _dbuf(p3, nc, "pnb", maxL, BF16)
        ptb = _dbuf(p3, nc, "ptb", 512, BF16)
        p3p = ExitStack()
        sps = p3p.enter_context(nc.psum_tensor("sps", [128, 2048], F32))
        ptp = p3p.enter_context(nc.psum_tensor("ptp", [128, 2048], BF16))
        avp = p3p.enter_context(nc.psum_tensor("avp", [128, 1024], F32))

        with nc.Block() as B:
            u_exp, u_norm, u_lastptr, u_av_last, u_ave = [], [], [], [], []
            chunk_exp = []
            p3_sEx_base = SM["sEx"].n
            av_marks = []
            pte_marks = []
            gchunk = [0]
            ggrp = [0]
            unit_info = []

            def emit_ptav(uidx):
                u, i, nch, L, kvh = unit_info[uidx]
                for grp in range(nch):
                    gg = ggrp[0]
                    gpar = gg % 2

                    @B.tensor
                    def _(te, u=u, grp=grp, gpar=gpar, gg=gg):
                        if grp == 0:
                            te.wait_ge(SM["vMi"].h, u_norm[u])
                            if u >= 2:
                                te.wait_ge(SM["vMi"].h, u_ave[u - 2])
                        if gg >= 2:
                            te.wait_ge(SM["vMi"].h, pte_marks[gg - 2])
                        last = None
                        for b in range(4):
                            last = te.transpose(
                                ptp[:, gpar * 1024 + b * 128:
                                    gpar * 1024 + (b + 1) * 128],
                                pnb(u)[:, (grp * 4 + b) * 128:
                                           (grp * 4 + b + 1) * 128],
                                ident[:, :])
                        SM["tPT"].inc(last)
                    ptr_mark = SM["tPT"].n

                    @B.vector
                    def _(v, gpar=gpar, gg=gg, ptr_mark=ptr_mark):
                        v.wait_ge(SM["tPT"].h, ptr_mark)
                        if gg >= 2:
                            v.wait_ge(SM["tAV"].h, av_marks[gg - 2])
                        pte_marks.append(SM["vMi"].inc(v.tensor_copy(
                            ptb(gpar)[:, :],
                            ptp[:, gpar * 1024:gpar * 1024 + 512])))

                    @B.tensor
                    def _(te, u=u, grp=grp, nch=nch, kvh=kvh, gpar=gpar, gg=gg):
                        te.wait_ge(SM["vMi"].h, pte_marks[gg])
                        last = None
                        for b in range(4):
                            kb = grp * 4 + b
                            last = te.matmul(
                                avp[:, (u % 2) * 512:(u % 2) * 512 + 128],
                                vsb[:, kb * KV + kvh * HD: kb * KV + kvh * HD + HD],
                                ptb(gpar)[:, b * 128:(b + 1) * 128],
                                start=(kb == 0), stop=(kb == nch * 4 - 1))
                        av_marks.append(SM["tAV"].inc(last))
                    ggrp[0] += 1
                u_lastptr.append(SM["tPT"].n)
                u_av_last.append(SM["tAV"].n)

                @B.vector
                def _(v, u=u, uidx=uidx):
                    v.wait_ge(SM["tAV"].h, u_av_last[uidx])
                    u_ave.append(SM["vMi"].inc(v.tensor_copy(
                        qf[:, u * 128:(u + 1) * 128],
                        avp[:, (u % 2) * 512:(u % 2) * 512 + 128])))

            for i in range(SLOTS):
                L = SLENS[i] * 128
                nch = SLENS[i] // 4
                for h in range(NH):
                    u = i * NH + h
                    kvh = h // REP
                    unit_info.append((u, i, nch, L, kvh))

                    @B.tensor
                    def _(te, u=u, i=i, nch=nch, kvh=kvh):
                        for cch in range(nch):
                            gc = gchunk[0]
                            if gc >= 4:
                                te.wait_ge(SM["sEx"].h, p3_sEx_base + gc - 3)
                            SM["tMM"].inc(te.matmul(
                                sps[:, (gc % 4) * 512:(gc % 4) * 512 + 512],
                                qf[:, u * 128:(u + 1) * 128],
                                kf[:, kvh * S + cch * 512: kvh * S + (cch + 1) * 512],
                                start=True, stop=True))
                            gchunk[0] += 1
                    mm_base = SM["tMM"].n - nch

                    @B.scalar
                    def _(sc, u=u, i=i, nch=nch, mm_base=mm_base):
                        if u >= 2:
                            sc.wait_ge(SM["tPT"].h, u_lastptr[u - 2])
                            sc.wait_ge(SM["vMi"].h, u_norm[u - 2])
                        for cch in range(nch):
                            gc = gchunk[0] - nch + cch
                            sc.wait_ge(SM["tMM"].h, mm_base + cch + 1)
                            dst = psb if cch < nch - 1 else pnb
                            acc = stat[:, (u % 2) * 16 + cch:(u % 2) * 16 + cch + 1]
                            chunk_exp.append(SM["sEx"].inc(sc.activation(
                                dst(u)[:, cch * 512:(cch + 1) * 512],
                                sps[:, (gc % 4) * 512:(gc % 4) * 512 + 512],
                                AF.Exp, bias=negM[:, u:u + 1], scale=1.0,
                                accum_out=acc)))
                        u_exp.append(SM["sEx"].n)

                    if u >= 1:
                        emit_ptav(u - 1)

                    @B.vector
                    def _(v, u=u, i=i, nch=nch, L=L):
                        v.wait_ge(SM["sEx"].h, u_exp[u])
                        b16 = (u % 2) * 16
                        w0 = (nch - 1) * 512
                        SM["vCh"].inc(v.tensor_tensor(
                            psb(u)[:, w0:w0 + 512],
                            pnb(u)[:, w0:w0 + 512], maskw[:, :], OP.mult))
                        v.wait_ge(SM["vCh"].h, SM["vCh"].n)
                        SM["vCh"].inc(v.reduce_sum(
                            stat[:, b16 + nch - 1:b16 + nch],
                            psb(u)[:, w0:w0 + 512], AX.X))
                        v.wait_ge(SM["vCh"].h, SM["vCh"].n)
                        SM["vCh"].inc(v.reduce_sum(stat2[:, u % 2:u % 2 + 1],
                                                   stat[:, b16:b16 + nch], AX.X))
                        v.wait_ge(SM["vCh"].h, SM["vCh"].n)
                        SM["vCh"].inc(v.reciprocal(stat2[:, 2 + u % 2:3 + u % 2],
                                                   stat2[:, u % 2:u % 2 + 1]))
                        v.wait_ge(SM["vCh"].h, SM["vCh"].n)
                        u_norm.append(SM["vMi"].inc(v.tensor_scalar(
                            pnb(u)[:, 0:L], psb(u)[:, 0:L],
                            stat2[:, 2 + u % 2:3 + u % 2], None, OP.mult)))

                    emit_tern(B, u * n_fac_gu // NUNITS,
                              (u + 1) * n_fac_gu // NUNITS)
            emit_ptav(NUNITS - 1)

        with nc.Block() as B:
            @B.gpsimd
            def _(g):
                g.wait_ge(SP["tf"][0].h, SP["tf"][0].n)
                g.wait_ge(SP["tf"][1].h, SP["tf"][1].n)
                g.wait_ge(SP["tf"][2].h, SP["tf"][2].n)
                g.wait_ge(SP["tf"][3].h, SP["tf"][3].n)
                g.wait_ge(SP["tfw"][0].h, SP["tfw"][0].n)
                g.wait_ge(SP["tfw"][1].h, SP["tfw"][1].n)

        p3p.close()
        p3.close()
        dump("attf", lambda: qf[:, :])
        dump("wsc", lambda: wsc[:, :])
    p13.close()
    aux_st.close()

    if upto >= 4:
        p45 = ExitStack()
        xqTf = p45.enter_context(nc.sbuf_tensor("xqTf", [128, SLOTS * D], BF16, side="right"))
        p4 = ExitStack()
        x1sl = _dbuf(p4, nc, "x1sl", 512, F32)
        x1full = _dbuf(p4, nc, "x1full", D, F32)
        wT4 = _dbuf(p4, nc, "wT4", DC * 512, BF16)
        xq32 = _dbuf(p4, nc, "xq32", 512, F32)
        x1n = _dbuf(p4, nc, "x1n", D, F32)
        q32a = p4.enter_context(nc.sbuf_tensor("q32a", [128, D], F32))
        q32b = p4.enter_context(nc.sbuf_tensor("q32b", [128, D], F32))
        q32c = p4.enter_context(nc.sbuf_tensor("q32c", [128, D], F32))
        q32i = p4.enter_context(nc.sbuf_tensor("q32i", [128, D], I32))
        sgnb = p4.enter_context(nc.sbuf_tensor("sgnb", [128, D], BF16))
        xqbt = p4.enter_context(nc.sbuf_tensor("xqbt", [128, D], BF16))
        p4p = ExitStack()
        tpa4 = p4p.enter_context(nc.psum_tensor("tpa4", [128, 1024], BF16))
        tpb4 = p4p.enter_context(nc.psum_tensor("tpb4", [128, 1024], BF16))
        ops4 = p4p.enter_context(nc.psum_tensor("ops4", [128, 1024], F32))

        with nc.Block() as B:
            o_mm, o_ev, xq_dma, x1_wr = [], [], [], []
            wt4_marks = []
            for oc in range(NOC):
                @B.scalar
                def _(s, oc=oc):
                    if oc == 0:
                        s.wait_ge(SM["cWO"].h, SM["cWO"].n)
                    if oc >= 2:
                        s.wait_ge(SM["tMM"].h, o_mm[-1])
                    for j in range(DC):
                        SP["wst4"][oc % 2].inc(s.dma_start(
                            wT4(oc)[:, j * 512:(j + 1) * 512],
                            wo_bf.ap()[oc * 512:(oc + 1) * 512,
                                       j * 128:(j + 1) * 128],
                            transpose=True), 16)
                wt4_marks.append(SP["wst4"][oc % 2].n)

                for i in range(SLOTS):
                    n_o = len(o_mm)

                    @B.sync
                    def _(s, oc=oc, i=i, n_o=n_o):
                        if n_o >= 2:
                            s.wait_ge(SM["vMi"].h, o_ev[n_o - 2])
                        xq_dma.append(SP["xq32"][n_o % 2].inc(s.dma_start(
                            xq32(n_o),
                            x_q.ap()[i * 128:(i + 1) * 128, oc * 512:(oc + 1) * 512]),
                            16))

                    @B.tensor
                    def _(te, oc=oc, i=i, n_o=n_o):
                        if i == 0:
                            te.wait_ge(SP["wst4"][oc % 2].h, wt4_marks[oc])
                        if n_o >= 2:
                            te.wait_ge(SM["vMi"].h, o_ev[n_o - 2])
                        sl = ops4[:, (n_o % 2) * 512:(n_o % 2) * 512 + 512]
                        for j in range(DC):
                            te.matmul(sl,
                                      qf[:, (i * NH + j) * 128:(i * NH + j + 1) * 128],
                                      wT4(oc)[:, j * 512:(j + 1) * 512],
                                      start=(j == 0), stop=False)
                        o_mm.append(SM["tMM"].inc(te.matmul(
                            sl, ones_bf[0:1, :],
                            bias_bf[0:1, D + oc * 512: D + (oc + 1) * 512],
                            start=False, stop=True)))

                    @B.vector
                    def _(v, oc=oc, i=i, n_o=n_o):
                        v.wait_ge(SM["tMM"].h, o_mm[n_o])
                        v.wait_ge(SP["xq32"][n_o % 2].h, xq_dma[n_o])
                        if n_o >= 2:
                            v.wait_ge(SP["x1rw"][n_o % 2].h, x1_wr[n_o - 2])
                        sl = ops4[:, (n_o % 2) * 512:(n_o % 2) * 512 + 512]
                        o_ev.append(SM["vMi"].inc(v.scalar_tensor_tensor(
                            x1sl(n_o), sl, 1.0, xq32(n_o), OP.mult, OP.add,
                            accum_out=stat[:, i * NOC + oc: i * NOC + oc + 1])))

                    @B.sync
                    def _(s, oc=oc, i=i, n_o=n_o):
                        s.wait_ge(SM["vMi"].h, o_ev[n_o])
                        x1_wr.append(SP["x1rw"][n_o % 2].inc(s.dma_start(
                            x1raw.ap()[i * 128:(i + 1) * 128,
                                       oc * 512:(oc + 1) * 512], x1sl(n_o)), 16))

        sl_spill = []
        sl_sign = []
        sl_tq = []
        sl_tmark = []
        sl_sq = []
        sl_qd = []
        x1w_tot = (SP["x1rw"][0].n, SP["x1rw"][1].n)
        x1_rdm = []
        with nc.Block() as B:
            for i in range(SLOTS):
                @B.scalar
                def _(sc, i=i):
                    if i == 0:
                        sc.wait_ge(SP["x1rw"][0].h, x1w_tot[0])
                        sc.wait_ge(SP["x1rw"][1].h, x1w_tot[1])
                    if i >= 2:
                        sc.wait_ge(SM["sEx"].h, sl_sq[i - 2])
                        sc.wait_ge(SM["vMi"].h, sl_qd[i - 2])
                    x1_rdm.append(SP["x1fr"][i % 2].inc(sc.dma_start(
                        x1full(i), x1raw.ap()[i * 128:(i + 1) * 128, :]), 16))

                @B.vector
                def _(v, i=i):
                    v.wait_ge(SP["x1fr"][i % 2].h, x1_rdm[i])
                    SM["vCh"].inc(v.reduce_sum(stat2[:, 16:17],
                                               stat[:, i * NOC:i * NOC + NOC], AX.X))
                    v.wait_ge(SM["vCh"].h, SM["vCh"].n)
                    SM["vCh"].inc(v.tensor_scalar(stat2[:, 17:18], stat2[:, 16:17],
                                                  1.0 / D, None, OP.mult))
                    v.wait_ge(SM["vCh"].h, SM["vCh"].n)
                    SM["vMi"].inc(v.engine_nop() if False else v.tensor_copy(
                        stat2[:, 27:28], stat2[:, 17:18]))
                m_done = SM["vMi"].n

                @B.scalar
                def _(sc, i=i, m_done=m_done):
                    sc.wait_ge(SM["vMi"].h, m_done)
                    sc.wait_ge(SP["x1fr"][i % 2].h, x1_rdm[i])
                    sl_sq.append(SM["sEx"].inc(sc.activation(
                        junk_bf[:, 0:D], x1full(i),
                        AF.Square, bias=0.0, scale=1.0,
                        accum_out=stat2[:, 18:19])))
                sq_done = SM["sEx"].n

                @B.vector
                def _(v, i=i, sq_done=sq_done):
                    v.wait_ge(SM["sEx"].h, sq_done)
                    v.tensor_scalar(stat2[:, 19:20], stat2[:, 18:19],
                                    1.0 / D, None, OP.mult)
                    SM["vCh"].inc(v.tensor_tensor(stat2[:, 20:21], stat2[:, 17:18],
                                                  stat2[:, 17:18], OP.mult))
                    v.wait_ge(SM["vCh"].h, SM["vCh"].n)
                    SM["vCh"].inc(v.tensor_tensor(stat2[:, 21:22], stat2[:, 19:20],
                                                  stat2[:, 20:21], OP.subtract))
                    v.wait_ge(SM["vCh"].h, SM["vCh"].n)
                    SM["vMi"].inc(v.tensor_scalar(stat2[:, 22:23], stat2[:, 21:22],
                                                  LN_EPS, None, OP.add))
                var_done = SM["vMi"].n

                @B.scalar
                def _(sc, var_done=var_done):
                    sc.wait_ge(SM["vMi"].h, var_done)
                    SM["sEx"].inc(sc.activation(stat2[:, 23:24], stat2[:, 22:23],
                                                AF.Sqrt, bias=0.0, scale=1.0))
                sd_done = SM["sEx"].n

                @B.vector
                def _(v, i=i, sd_done=sd_done):
                    v.wait_ge(SM["sEx"].h, sd_done)
                    if i >= 2:
                        v.wait_ge(SM["sEx"].h, sl_sign[i - 2])
                        v.wait_ge(SP["x1sp"][i % 2].h, sl_spill[i - 2])
                    SM["vCh"].inc(v.reciprocal(stat2[:, 24:25], stat2[:, 23:24]))
                    v.wait_ge(SM["vCh"].h, SM["vCh"].n)
                    SM["vCh"].inc(v.tensor_scalar(stat2[:, 25:26], stat2[:, 17:18],
                                                  -1.0, None, OP.mult))
                    v.wait_ge(SM["vCh"].h, SM["vCh"].n)
                    SM["vCh"].inc(v.tensor_scalar(x1n(i), x1full(i),
                                                  stat2[:, 25:26], stat2[:, 24:25],
                                                  OP.add, OP.mult))
                    v.wait_ge(SM["vCh"].h, SM["vCh"].n)
                    SM["vCh"].inc(v.reduce_max(stat2[:, 26:27], x1n(i), AX.X,
                                               apply_absolute_value=True))
                    v.wait_ge(SM["vCh"].h, SM["vCh"].n)
                    SM["vCh"].inc(v.tensor_scalar(scl["s_x"][:, i:i + 1],
                                                  stat2[:, 26:27], QEPS, 1.0 / 127.0,
                                                  OP.max, OP.mult))
                    v.wait_ge(SM["vCh"].h, SM["vCh"].n)
                    SM["vCh"].inc(v.reciprocal(scl["r_x"][:, i:i + 1],
                                               scl["s_x"][:, i:i + 1]))
                    v.wait_ge(SM["vCh"].h, SM["vCh"].n)
                    if i >= 1:
                        v.wait_ge(SM["tXT"].h, sl_tmark[i - 1])
                    SM["vMi"].inc(v.nop())
                a_rdy = SM["vMi"].n

                @B.scalar
                def _(sc, i=i, a_rdy=a_rdy):
                    sc.wait_ge(SM["vMi"].h, a_rdy)
                    sc.activation(sgnb[:, :], x1n(i), AF.Sign, bias=0.0, scale=1.0)
                    sl_sign.append(SM["sEx"].inc(sc.activation(
                        q32a[:, :], x1n(i), AF.Abs, bias=0.0,
                        scale=scl["r_x"][:, i:i + 1])))
                abs_done = SM["sEx"].n

                @B.vector
                def _(v, i=i, abs_done=abs_done):
                    v.wait_ge(SM["sEx"].h, abs_done)
                    SM["vCh"].inc(v.tensor_scalar(q32b[:, :], q32a[:, :], 0.5,
                                                  None, OP.add))
                    v.wait_ge(SM["vCh"].h, SM["vCh"].n)
                    SM["vCh"].inc(v.tensor_copy(q32i[:, :], q32b[:, :]))
                    v.wait_ge(SM["vCh"].h, SM["vCh"].n)
                    SM["vCh"].inc(v.tensor_copy(q32c[:, :], q32i[:, :]))
                    v.wait_ge(SM["vCh"].h, SM["vCh"].n)
                    SM["vCh"].inc(v.tensor_tensor(q32a[:, :], q32c[:, :], q32b[:, :],
                                                  OP.is_gt))
                    v.wait_ge(SM["vCh"].h, SM["vCh"].n)
                    SM["vMi"].inc(v.tensor_tensor(q32b[:, :], q32c[:, :], q32a[:, :],
                                                  OP.subtract))
                q_done = SM["vMi"].n
                sl_qd.append(q_done)

                @B.sync
                def _(s, i=i, q_done=q_done):
                    s.wait_ge(SM["vMi"].h, q_done)
                    sl_spill.append(SP["x1sp"][i % 2].inc(s.dma_start(
                        x1scr.ap()[i * 128:(i + 1) * 128, :], x1n(i)), 16))

                @B.vector
                def _(v, i=i, q_done=q_done):
                    v.wait_ge(SM["sEx"].h, sl_sign[i])
                    v.wait_ge(SM["vMi"].h, q_done)
                    sl_tq.append(SM["vMi"].inc(v.tensor_tensor(
                        xqbt[:, :], q32b[:, :], sgnb[:, :], OP.mult)))

                @B.tensor
                def _(te, i=i):
                    te.wait_ge(SM["vMi"].h, sl_tq[i])
                    for j in range(DC):
                        bank = (tpa4, tpb4)[(j // 8) % 2]
                        if j % 8 == 0:
                            te.wait_ge(SM["sEv"].h, bank_free.get(id(bank), 0))
                        ins = te.transpose(bank[:, (j % 8) * 128:(j % 8 + 1) * 128],
                                           xqbt[:, j * 128:(j + 1) * 128], ident[:, :])
                        if j % 8 == 7 or j == DC - 1:
                            SM["tXT"].inc(ins)
                sl_tmark.append(SM["tXT"].n)
                ngrp = (DC + 7) // 8
                t_mark = SM["tXT"].n

                @B.vector
                def _(v, i=i, t_mark=t_mark, ngrp=ngrp):
                    for gi in range(ngrp):
                        bank = (tpa4, tpb4)[gi % 2]
                        v.wait_ge(SM["tXT"].h, t_mark - ngrp + 1 + gi)
                        lo = gi * 8
                        hi = min(lo + 8, DC)
                        dst = xqTf[:].rearrange("p (j k) -> p j k", j=DC) \
                            [:, lo:hi, i * 128:(i + 1) * 128]
                        SM["sEv"].inc(v.tensor_copy(dst, bank[:, 0:(hi - lo) * 128]))
                        bank_free[id(bank)] = SM["sEv"].n
        with nc.Block() as B:
            @B.sync
            def _(s):
                s.wait_ge(SP["x1sp"][0].h, SP["x1sp"][0].n)
                s.wait_ge(SP["x1sp"][1].h, SP["x1sp"][1].n)
        p4p.close()
        p4.close()
        dump("xqTf", lambda: xqTf[:, :])
        dump("sx", lambda: scl["s_x"][:, :])

    attn_st.close()
    if upto >= 5:
        p5 = ExitStack()
        fac_alloc(p5, "b")
        sg2 = p5.enter_context(nc.sbuf_tensor("sg2", [128, 512], F32))
        wTg = _dbuf(p5, nc, "wTg", DC * 512, BF16, n=3)
        wTu = _dbuf(p5, nc, "wTu", DC * 512, BF16, n=3)
        sgb = p5.enter_context(nc.sbuf_tensor("sgb", [128, 512], F32))
        hb5 = _dbuf(p5, nc, "hb5", 512, F32)
        p5p = ExitStack()
        gps = p5p.enter_context(nc.psum_tensor("gps", [128, 2048], F32))
        ups = p5p.enter_context(nc.psum_tensor("ups", [128, 2048], F32))

        with nc.Block() as B:
            @B.vector
            def _(v):
                v.tensor_scalar(scl["sgx"][:, :], scl["s_x"][:, :],
                                wsc[:, 0:1], None, OP.mult)
                SM["vCh"].inc(v.tensor_scalar(scl["sux"][:, :], scl["s_x"][:, :],
                                              wsc[:, 1:2], None, OP.mult))
                SM["vMi"].inc(v.nop())
                v.wait_ge(SM["vCh"].h, SM["vCh"].n)

        h_wr = {}
        sg_done = {}
        n_wd_fac = n_fac - n_fac_gu
        with nc.Block() as B:
            gu_mm, gu_ev, chunk_mm, w_ready = [], [], [], {}
            pfree = []
            tpc = 512 // 128
            nwu = DFF // 128

            def emit_wload(c):
                @B.sync
                def _(s, c=c):
                    th = (c + 1) * tpc - 1
                    s.wait_ge(SP["tfw"][th % 2].h, fac_wb[th])
                    s.wait_ge(SP["tfw"][(th - 1) % 2].h, fac_wb[th - 1])
                    if c >= 3:
                        s.wait_ge(SM["tMM"].h, chunk_mm[c - 3])
                    for j in range(DC):
                        SP["w5st"][c % 3].inc(s.dma_start(
                            wTg(c)[:, j * 512:(j + 1) * 512],
                            wg_q.ap()[c * 512:(c + 1) * 512,
                                      j * 128:(j + 1) * 128],
                            transpose=True), 16)

                @B.sync
                def _(s, c=c):
                    th = nwu + (c + 1) * tpc - 1
                    s.wait_ge(SP["tfw"][th % 2].h, fac_wb[th])
                    s.wait_ge(SP["tfw"][(th - 1) % 2].h, fac_wb[th - 1])
                    if c >= 3:
                        s.wait_ge(SM["tMM"].h, chunk_mm[c - 3])
                    for j in range(DC):
                        SP["w5st"][3 + c % 3].inc(s.dma_start(
                            wTu(c)[:, j * 512:(j + 1) * 512],
                            wu_q.ap()[c * 512:(c + 1) * 512,
                                      j * 128:(j + 1) * 128],
                            transpose=True), 16)
                w_ready[c] = (SP["w5st"][c % 3].n, SP["w5st"][3 + c % 3].n)

            for c0 in range(min(3, FFC)):
                emit_wload(c0)
            for c in range(FFC):
                for tt in range(SLOTS):
                    n_g = len(gu_mm)

                    @B.tensor
                    def _(te, c=c, tt=tt, n_g=n_g):
                        if tt == 0:
                            te.wait_ge(SP["w5st"][c % 3].h, w_ready[c][0])
                            te.wait_ge(SP["w5st"][3 + c % 3].h, w_ready[c][1])
                        if n_g >= 4:
                            te.wait_ge(SM["vMi"].h, pfree[n_g - 4])
                        qo = (n_g % 4) * 512
                        last = None
                        for j in range(DC):
                            lhs = xqTf[:, j * NTOK + tt * 128:
                                       j * NTOK + (tt + 1) * 128]
                            te.matmul(gps[:, qo:qo + 512], lhs,
                                      wTg(c)[:, j * 512:(j + 1) * 512],
                                      start=(j == 0), stop=(j == DC - 1))
                            last = te.matmul(ups[:, qo:qo + 512], lhs,
                                             wTu(c)[:, j * 512:(j + 1) * 512],
                                             start=(j == 0), stop=(j == DC - 1))
                        gu_mm.append(SM["tMM"].inc(last))

                    @B.vector
                    def _(v, tt=tt, n_g=n_g):
                        v.wait_ge(SM["tMM"].h, gu_mm[n_g])
                        if n_g >= 1:
                            v.wait_ge(SM["sEx"].h, sg_done[n_g - 1])
                        qo = (n_g % 4) * 512
                        SM["vCh"].inc(v.tensor_scalar(sgb[:, :],
                                                      gps[:, qo:qo + 512],
                                                      scl["sgx"][:, tt:tt + 1],
                                                      None, OP.mult))
                    evg_mark = SM["vCh"].n

                    @B.scalar
                    def _(sc, n_g=n_g, evg_mark=evg_mark):
                        sc.wait_ge(SM["vCh"].h, evg_mark)
                        if n_g >= 1:
                            sc.wait_ge(SM["vMi"].h, pfree[n_g - 1])
                        sg_done[n_g] = SM["sEx"].inc(sc.activation(
                            sg2[:, :], sgb[:, :], AF.Sigmoid, bias=0.0, scale=1.0))

                    idx = c * SLOTS + tt
                    n_it = FFC * SLOTS
                    emit_tern(B, n_fac_gu + idx * n_wd_fac // n_it,
                              n_fac_gu + (idx + 1) * n_wd_fac // n_it)

                    @B.vector
                    def _(v, c=c, tt=tt, n_g=n_g):
                        qo = (n_g % 4) * 512
                        if n_g >= 2:
                            v.wait_ge(SP["hwr"][n_g % 2].h, h_wr.get(n_g - 2, 0))
                        v.wait_ge(SM["sEx"].h, sg_done[n_g])
                        pfree.append(SM["vMi"].inc(v.scalar_tensor_tensor(
                            hb5(n_g), ups[:, qo:qo + 512],
                            scl["sux"][:, tt:tt + 1], sg2[:, :],
                            OP.mult, OP.mult)))
                        v.wait_ge(SM["vMi"].h, pfree[-1])
                        SM["vCh"].inc(v.reduce_max(stat2[:, 30:31], hb5(n_g),
                                                   AX.X, apply_absolute_value=True))
                        v.wait_ge(SM["vCh"].h, SM["vCh"].n)
                        gu_ev.append(SM["vMi"].inc(v.tensor_tensor(
                            stat[:, 32 + tt:33 + tt], stat[:, 32 + tt:33 + tt],
                            stat2[:, 30:31], OP.max)))

                    @B.gpsimd
                    def _(g, c=c, tt=tt, n_g=n_g):
                        g.wait_ge(SM["vMi"].h, pfree[n_g])
                        h_wr[n_g] = SP["hwr"][n_g % 2].inc(g.dma_start(
                            hscr.ap()[tt * 128:(tt + 1) * 128, c * 512:(c + 1) * 512],
                            hb5(n_g)), 16)
                chunk_mm.append(SM["tMM"].n)
                if c + 3 < FFC:
                    emit_wload(c + 3)

        with nc.Block() as B:
            @B.vector
            def _(v):
                SM["vCh"].inc(v.tensor_scalar(scl["s_h"][:, :], stat[:, 32:32 + SLOTS],
                                              QEPS, 1.0 / 127.0, OP.max, OP.mult))
                v.wait_ge(SM["vCh"].h, SM["vCh"].n)
                SM["vCh"].inc(v.reciprocal(scl["r_h"][:, :], scl["s_h"][:, :]))
                v.wait_ge(SM["vCh"].h, SM["vCh"].n)
                SM["vCh"].inc(v.tensor_scalar(scl["shd"][:, :], scl["s_h"][:, :],
                                              wsc[:, 2:3], None, OP.mult))
                v.wait_ge(SM["vCh"].h, SM["vCh"].n)
        with nc.Block() as B:
            @B.scalar
            def _(sc):
                sc.wait_ge(SP["hwr"][0].h, SP["hwr"][0].n)
                sc.wait_ge(SP["hwr"][1].h, SP["hwr"][1].n)

            @B.gpsimd
            def _(g):
                g.wait_ge(SP["tf"][0].h, SP["tf"][0].n)
                g.wait_ge(SP["tf"][1].h, SP["tf"][1].n)
                g.wait_ge(SP["tf"][2].h, SP["tf"][2].n)
                g.wait_ge(SP["tf"][3].h, SP["tf"][3].n)
                g.wait_ge(SP["tfw"][0].h, SP["tfw"][0].n)
                g.wait_ge(SP["tfw"][1].h, SP["tfw"][1].n)
        p5p.close()
        p5.close()
        p45.close()
        dump("sh", lambda: scl["s_h"][:, :])

    if upto >= 6:
        p67 = ExitStack()
        acc6 = p67.enter_context(nc.sbuf_tensor("acc6", [128, SLOTS * D], F32))
        p6 = ExitStack()
        hqT = _dbuf(p6, nc, "hqT", 8 * NTOK, BF16)
        wdT = _dbuf(p6, nc, "wdT", 8 * D, BF16)
        hst6 = _dbuf(p6, nc, "hst6", 1024, F32)
        h32a = p6.enter_context(nc.sbuf_tensor("h32a", [128, 1024], F32, side="right"))
        h32b = p6.enter_context(nc.sbuf_tensor("h32b", [128, 1024], F32, side="right"))
        h32c = p6.enter_context(nc.sbuf_tensor("h32c", [128, 1024], F32, side="right"))
        h32i = p6.enter_context(nc.sbuf_tensor("h32i", [128, 1024], I32, side="right"))
        sg6 = p6.enter_context(nc.sbuf_tensor("sg6", [128, 1024], BF16, side="right"))
        hqb = p6.enter_context(nc.sbuf_tensor("hqb", [128, 1024], BF16))
        p6p = ExitStack()
        tpa6 = p6p.enter_context(nc.psum_tensor("tpa6", [128, 1024], BF16))
        tpb6 = p6p.enter_context(nc.psum_tensor("tpb6", [128, 1024], BF16))
        dn6 = p6p.enter_context(nc.psum_tensor("dn6", [128, 2048], F32))

        hwr_tot = (SP["hwr"][0].n, SP["hwr"][1].n)
        with nc.Block() as B:
            grp_mm_last = []
            hq_ready, hq_ts, hq_cp = {}, {}, {}

            def emit_hq(gidx, tt):
                n_h = gidx * SLOTS + tt
                ffb = gidx * 1024

                @B.scalar
                def _(sc, gidx=gidx, tt=tt, n_h=n_h, ffb=ffb):
                    if n_h == 0:
                        sc.wait_ge(SP["hwr"][0].h, hwr_tot[0])
                        sc.wait_ge(SP["hwr"][1].h, hwr_tot[1])
                    if n_h >= 2:
                        sc.wait_ge(SM["vMi"].h, hq_q.get(n_h - 2, 0))
                        sc.wait_ge(SM["sEx"].h, hq_sg.get(n_h - 2, 0))
                    hq_dma[n_h] = SP["hst6"][n_h % 2].inc(sc.dma_start(
                        hst6(n_h),
                        hscr.ap()[tt * 128:(tt + 1) * 128, ffb:ffb + 1024]), 16)

                @B.scalar
                def _(sc, tt=tt, n_h=n_h):
                    sc.wait_ge(SP["hst6"][n_h % 2].h, hq_dma[n_h])
                    if n_h >= 1:
                        sc.wait_ge(SM["vMi"].h, hq_bf.get(n_h - 1, 0))
                    sc.activation(sg6[:, :], hst6(n_h), AF.Sign,
                                  bias=0.0, scale=1.0)
                    hq_sg[n_h] = SM["sEx"].inc(sc.activation(
                        h32a[:, :], hst6(n_h), AF.Abs, bias=0.0,
                        scale=scl["r_h"][:, tt:tt + 1]))

                @B.vector
                def _(v, n_h=n_h):
                    v.wait_ge(SM["sEx"].h, hq_sg[n_h])
                    if n_h >= 1:
                        v.wait_ge(SM["sEx"].h, hq_cp.get(n_h - 1, 0))
                    hq_ts[n_h] = SM["vCh"].inc(v.tensor_scalar(
                        h32b[:, :], h32a[:, :], 0.5, None, OP.add))

                @B.scalar
                def _(sc, n_h=n_h):
                    sc.wait_ge(SM["vCh"].h, hq_ts[n_h])
                    if n_h >= 1:
                        sc.wait_ge(SM["vMi"].h, hq_q.get(n_h - 1, 0))
                    SM["sEx"].inc(sc.copy(h32i[:, :], h32b[:, :]))
                    sc.wait_ge(SM["sEx"].h, SM["sEx"].n)
                    hq_cp[n_h] = SM["sEx"].inc(sc.copy(h32c[:, :], h32i[:, :]))

                @B.vector
                def _(v, n_h=n_h):
                    v.wait_ge(SM["sEx"].h, hq_cp[n_h])
                    if n_h >= 1:
                        v.wait_ge(SM["tXT"].h, hq_tr_mark.get(n_h - 1, 0))
                    SM["vCh"].inc(v.tensor_tensor(h32a[:, :], h32c[:, :],
                                                  h32b[:, :], OP.is_gt))
                    v.wait_ge(SM["vCh"].h, SM["vCh"].n)
                    hq_q[n_h] = SM["vMi"].inc(v.tensor_tensor(
                        h32b[:, :], h32c[:, :], h32a[:, :], OP.subtract))
                    v.wait_ge(SM["vMi"].h, hq_q[n_h])
                    hq_bf[n_h] = SM["vMi"].inc(v.tensor_tensor(
                        hqb[:, :], h32b[:, :], sg6[:, :], OP.mult))

                @B.tensor
                def _(te, gidx=gidx, tt=tt, n_h=n_h):
                    te.wait_ge(SM["vMi"].h, hq_bf[n_h])
                    if gidx >= 2 and tt == 0:
                        te.wait_ge(SM["tMM"].h, grp_mm_last[gidx - 2])
                    bank = (tpa6, tpb6)[n_h % 2]
                    te.wait_ge(SM["sEv"].h, bank_free.get(id(bank), 0))
                    for j in range(8):
                        ins = te.transpose(bank[:, j * 128:(j + 1) * 128],
                                           hqb[:, j * 128:(j + 1) * 128],
                                           ident[:, :])
                    SM["tXT"].inc(ins)
                    hq_tr_mark[n_h] = SM["tXT"].n
                t_mark = SM["tXT"].n

                @B.vector
                def _(v, gidx=gidx, tt=tt, n_h=n_h, t_mark=t_mark):
                    bank = (tpa6, tpb6)[n_h % 2]
                    v.wait_ge(SM["tXT"].h, t_mark)
                    dst = hqT(gidx).rearrange("p (j k) -> p j k", j=8) \
                        [:, 0:8, tt * 128:(tt + 1) * 128]
                    SM["sEv"].inc(v.tensor_copy(dst, bank[:, 0:1024]))
                    bank_free[id(bank)] = SM["sEv"].n
                if tt == SLOTS - 1:
                    hq_ready[gidx] = SM["sEv"].n

            def emit_wdload(gidx):
                ffb = gidx * 1024

                @B.sync
                def _(s, gidx=gidx, ffb=ffb):
                    if gidx == 0:
                        s.wait_ge(SP["tfw"][(n_fac - 1) % 2].h, fac_wb[n_fac - 1])
                        s.wait_ge(SP["tfw"][(n_fac - 2) % 2].h, fac_wb[n_fac - 2])
                    if gidx >= 2:
                        s.wait_ge(SM["tMM"].h, grp_mm_last[gidx - 2])
                    for cf8 in range(8):
                        SP["wd6"][gidx % 2].inc(s.dma_start(
                            wdT(gidx)[:, cf8 * D:(cf8 + 1) * D],
                            wd_q.ap()[0:D, ffb + cf8 * 128:
                                      ffb + (cf8 + 1) * 128],
                            transpose=True), 16)
                return SP["wd6"][gidx % 2].n

            wd_ready = {0: emit_wdload(0)}
            for tt in range(SLOTS):
                emit_hq(0, tt)
            if NGR > 1:
                wd_ready[1] = emit_wdload(1)

            assert NOC <= 4
            for gidx in range(NGR):
                for tt in range(SLOTS):
                    n_c = gidx * SLOTS + tt

                    @B.tensor
                    def _(te, gidx=gidx, tt=tt, n_c=n_c):
                        te.wait_ge(SM["sEv"].h, hq_ready[gidx])
                        if tt == 0:
                            te.wait_ge(SP["wd6"][gidx % 2].h, wd_ready[gidx])
                        if n_c >= 1:
                            te.wait_ge(SM["vMi"].h, dn_ev.get(n_c - 1, 0))
                        last = None
                        for cf8 in range(8):
                            lhs = hqT(gidx)[:, cf8 * NTOK + tt * 128:
                                            cf8 * NTOK + (tt + 1) * 128]
                            for oc2 in range(NOC):
                                last = te.matmul(
                                    dn6[:, oc2 * 512:(oc2 + 1) * 512], lhs,
                                    wdT(gidx)[:, cf8 * D + oc2 * 512:
                                              cf8 * D + (oc2 + 1) * 512],
                                    start=(cf8 == 0), stop=(cf8 == 7))
                        dn_mm[n_c] = SM["tMM"].inc(last)

                    @B.vector
                    def _(v, gidx=gidx, tt=tt, n_c=n_c):
                        v.wait_ge(SM["tMM"].h, dn_mm[n_c])
                        last = None
                        for oc2 in range(NOC):
                            dst = acc6[:, tt * D + oc2 * 512:
                                       tt * D + (oc2 + 1) * 512]
                            if gidx == 0:
                                last = v.tensor_scalar(
                                    dst, dn6[:, oc2 * 512:(oc2 + 1) * 512],
                                    scl["shd"][:, tt:tt + 1], None, OP.mult)
                            else:
                                last = v.scalar_tensor_tensor(
                                    dst, dn6[:, oc2 * 512:(oc2 + 1) * 512],
                                    scl["shd"][:, tt:tt + 1], dst, OP.mult, OP.add)
                        dn_ev[n_c] = SM["vMi"].inc(last)

                    if gidx + 1 < NGR:
                        emit_hq(gidx + 1, tt)
                grp_mm_last.append(SM["tMM"].n)
                if gidx + 2 < NGR:
                    wd_ready[gidx + 2] = emit_wdload(gidx + 2)

        p6p.close()
        p6.close()
        dump("acc6", lambda: acc6[:, :])
        if upto < 7:
            p67.close()

    if upto >= 7:
        p7 = ExitStack()
        x1n2 = _dbuf(p7, nc, "x1n2", D, F32)
        x2b7 = _dbuf(p7, nc, "x2b7", D, F32)
        outb = _dbuf(p7, nc, "outb", D, F32)
        x1_rd, r2_done, sq7, out_dma = {}, {}, {}, {}
        sp_tot = (SP["x1sp"][0].n, SP["x1sp"][1].n)
        with nc.Block() as B:
            for tt in range(SLOTS):
                @B.scalar
                def _(sc, tt=tt):
                    if tt == 0:
                        sc.wait_ge(SP["x1sp"][0].h, sp_tot[0])
                        sc.wait_ge(SP["x1sp"][1].h, sp_tot[1])
                    if tt >= 2:
                        sc.wait_ge(SM["sEx"].h, sq7[tt - 2])
                    x1_rd[tt] = SP["x1rd"][tt % 2].inc(sc.dma_start(
                        x1n2(tt), x1scr.ap()[tt * 128:(tt + 1) * 128, :]), 16)

                @B.vector
                def _(v, tt=tt):
                    v.wait_ge(SP["x1rd"][tt % 2].h, x1_rd[tt])
                    if tt >= 2:
                        v.wait_ge(SP["outw"][tt % 2].h, out_dma[tt - 2])
                    r2_done[tt] = SM["vMi"].inc(v.scalar_tensor_tensor(
                        x2b7(tt), acc6[:, tt * D:(tt + 1) * D], 1.0,
                        x1n2(tt), OP.mult, OP.add,
                        accum_out=stat2[:, 40:41]))

                @B.scalar
                def _(sc, tt=tt):
                    sc.wait_ge(SM["vMi"].h, r2_done[tt])
                    sq7[tt] = SM["sEx"].inc(sc.activation(
                        junk_bf[:, 0:D], x2b7(tt), AF.Square, bias=0.0,
                        scale=1.0, accum_out=stat2[:, 41:42]))

                @B.vector
                def _(v, tt=tt):
                    v.wait_ge(SM["sEx"].h, sq7[tt])
                    v.tensor_scalar(stat2[:, 42:43], stat2[:, 40:41],
                                    1.0 / D, None, OP.mult)
                    SM["vCh"].inc(v.tensor_scalar(stat2[:, 43:44], stat2[:, 41:42],
                                                  1.0 / D, None, OP.mult))
                    v.wait_ge(SM["vCh"].h, SM["vCh"].n)
                    SM["vCh"].inc(v.tensor_tensor(stat2[:, 44:45], stat2[:, 42:43],
                                                  stat2[:, 42:43], OP.mult))
                    v.wait_ge(SM["vCh"].h, SM["vCh"].n)
                    SM["vCh"].inc(v.tensor_tensor(stat2[:, 45:46], stat2[:, 43:44],
                                                  stat2[:, 44:45], OP.subtract))
                    v.wait_ge(SM["vCh"].h, SM["vCh"].n)
                    SM["vMi"].inc(v.tensor_scalar(stat2[:, 46:47], stat2[:, 45:46],
                                                  LN_EPS, None, OP.add))
                var7 = SM["vMi"].n

                @B.scalar
                def _(sc, var7=var7):
                    sc.wait_ge(SM["vMi"].h, var7)
                    SM["sEx"].inc(sc.activation(stat2[:, 47:48], stat2[:, 46:47],
                                                AF.Sqrt, bias=0.0, scale=1.0))
                sd7 = SM["sEx"].n

                @B.vector
                def _(v, tt=tt, sd7=sd7):
                    v.wait_ge(SM["sEx"].h, sd7)
                    SM["vCh"].inc(v.reciprocal(stat2[:, 48:49], stat2[:, 47:48]))
                    v.wait_ge(SM["vCh"].h, SM["vCh"].n)
                    SM["vCh"].inc(v.tensor_scalar(stat2[:, 49:50], stat2[:, 42:43],
                                                  -1.0, None, OP.mult))
                    v.wait_ge(SM["vCh"].h, SM["vCh"].n)
                    SM["vMi"].inc(v.tensor_scalar(outb(tt), x2b7(tt),
                                                  stat2[:, 49:50], stat2[:, 48:49],
                                                  OP.add, OP.mult))
                o7 = SM["vMi"].n

                @B.sync
                def _(s, tt=tt, o7=o7):
                    s.wait_ge(SM["vMi"].h, o7)
                    out_dma[tt] = SP["outw"][tt % 2].inc(s.dma_start(
                        out_d.ap()[tt * 128:(tt + 1) * 128, :], outb(tt)), 16)
        p7.close()
        p67.close()

    with nc.Block() as B:
        @B.sync
        def _(s):
            s.wait_ge(SP["outw"][0].h, SP["outw"][0].n)
            s.wait_ge(SP["outw"][1].h, SP["outw"][1].n)
            s.wait_ge(SM["dDbg"].h, SM["dDbg"].n)
    st.close()
    return nc


def _prep_inputs(cfg, inputs, core):
    D, SLOTS, NB = cfg["D"], cfg["SLOTS"], cfg["NB"]
    b, k = core // 4, core % 4
    x = np.asarray(inputs["L0_attn_x"], dtype=np.float32)
    blocks = [4 * i + k for i in range(SLOTS)]
    xb = x[b]
    x_q = np.concatenate([xb[g * 128:(g + 1) * 128] for g in blocks], axis=0)
    qpos = np.empty((128, SLOTS), np.float32)
    for i, g in enumerate(blocks):
        qpos[:, i] = g * 128 + np.arange(128)
    thr = (128.0 * k + np.arange(128) + 1.0).astype(np.float32).reshape(128, 1)
    m = {
        "x_q": np.ascontiguousarray(x_q),
        "x_kv": np.ascontiguousarray(xb),
        "wq": np.asarray(inputs["L0_attn_Wq"], np.float32),
        "bq": np.asarray(inputs["L0_attn_bq"], np.float32).reshape(1, -1),
        "wk": np.asarray(inputs["L0_attn_Wk"], np.float32),
        "bk": np.asarray(inputs["L0_attn_bk"], np.float32).reshape(1, -1),
        "wv": np.asarray(inputs["L0_attn_Wv"], np.float32),
        "bv": np.asarray(inputs["L0_attn_bv"], np.float32).reshape(1, -1),
        "wo": np.asarray(inputs["L0_attn_Wo"], np.float32),
        "bo": np.asarray(inputs["L0_attn_bo"], np.float32).reshape(1, -1),
        "wg": np.asarray(inputs["L0_ffn_w_gate"], np.float32),
        "wu": np.asarray(inputs["L0_ffn_w_up"], np.float32),
        "wd": np.asarray(inputs["L0_ffn_w_down"], np.float32),
        "qpos": qpos,
        "thr": thr,
    }
    return m


def kernel(**inputs):
    cfg = real_cfg()
    D, SLOTS = cfg["D"], cfg["SLOTS"]
    nc = build_nc(cfg)
    in_maps = [_prep_inputs(cfg, inputs, c) for c in range(8)]
    res = run_bass_kernel_spmd(nc, in_maps, core_ids=list(range(8)))
    x = np.asarray(inputs["L0_attn_x"], dtype=np.float32)
    out = np.empty_like(x)
    for c in range(8):
        b, k = c // 4, c % 4
        oc = np.asarray(res.results[c]["out"])
        for i in range(SLOTS):
            g = 4 * i + k
            out[b, g * 128:(g + 1) * 128] = oc[i * 128:(i + 1) * 128]
    return out


# version 14
# speedup vs baseline: 1.0453x; 1.0453x over previous
import math
from contextlib import ExitStack

import numpy as np

import concourse.bass as bass
import concourse.mybir as mybir
from concourse.bass_utils import run_bass_kernel_spmd

F32 = mybir.dt.float32
I32 = mybir.dt.int32
BF16 = mybir.dt.bfloat16
AF = mybir.ActivationFunctionType
OP = mybir.AluOpType
AX = mybir.AxisListType

INV2PI = 1.0 / (2.0 * math.pi)
ROPE_BASE = 10000.0
LN_EPS = 1e-5
QEPS = 1e-5


def real_cfg():
    return dict(D=2048, DFF=8192, HD=128, NH=16, NKV=4, NB=32, SLOTS=8,
                SLENS=(4, 8, 12, 16, 20, 24, 28, 32))


def small_cfg():
    return dict(D=512, DFF=1024, HD=128, NH=4, NKV=2, NB=8, SLOTS=2,
                SLENS=(4, 8))


class Sem:
    __slots__ = ("h", "n")

    def __init__(self, nc, st, name):
        self.h = st.enter_context(nc.semaphore(name))
        self.n = 0

    def inc(self, instr, k=1):
        instr.then_inc(self.h, k)
        self.n += k
        return self.n


hq_dma, hq_q, hq_sg, hq_bf, hq_tr_mark = {}, {}, {}, {}, {}
wd_dma, wd_cmp, wd_add, wd_tr = {}, {}, {}, {}
dn_mm, dn_ev = {}, {}


def _dbuf(stack, nc, name, cols, dtype, n=2):
    t = stack.enter_context(nc.sbuf_tensor(name, [128, n * cols], dtype))
    return lambda p: t[:, (p % n) * cols:((p % n) + 1) * cols]


def build_nc(cfg, upto=99, dbg=()):
    D, DFF, HD = cfg["D"], cfg["DFF"], cfg["HD"]
    NH, NKV, NB = cfg["NH"], cfg["NKV"], cfg["NB"]
    SLOTS, SLENS = cfg["SLOTS"], cfg["SLENS"]
    KV = NKV * HD
    DC = D // 128
    HJ = HD // 2
    NTOK = SLOTS * 128
    S = NB * 128
    NOC = D // 512
    HPO = 512 // HD
    FFC = DFF // 512
    NGR = DFF // 1024
    NUNITS = SLOTS * NH
    REP = NH // NKV
    scale_q = 1.0 / math.sqrt(HD)
    maxL = SLENS[-1] * 128
    assert NH == DC

    for dd in (hq_dma, hq_q, hq_sg, hq_bf, hq_tr_mark, wd_dma, wd_cmp,
               wd_add, wd_tr, dn_mm, dn_ev):
        dd.clear()
    nc = bass.Bass("TRN2", target_bir_lowering=False)
    st = ExitStack()

    x_q = nc.dram_tensor("x_q", [NTOK, D], F32, kind="ExternalInput")
    x_kv = nc.dram_tensor("x_kv", [S, D], F32, kind="ExternalInput")
    wq_d = nc.dram_tensor("wq", [D, D], F32, kind="ExternalInput")
    bq_d = nc.dram_tensor("bq", [1, D], F32, kind="ExternalInput")
    wk_d = nc.dram_tensor("wk", [KV, D], F32, kind="ExternalInput")
    bk_d = nc.dram_tensor("bk", [1, KV], F32, kind="ExternalInput")
    wv_d = nc.dram_tensor("wv", [KV, D], F32, kind="ExternalInput")
    bv_d = nc.dram_tensor("bv", [1, KV], F32, kind="ExternalInput")
    wo_d = nc.dram_tensor("wo", [D, D], F32, kind="ExternalInput")
    bo_d = nc.dram_tensor("bo", [1, D], F32, kind="ExternalInput")
    wg_d = nc.dram_tensor("wg", [DFF, D], F32, kind="ExternalInput")
    wu_d = nc.dram_tensor("wu", [DFF, D], F32, kind="ExternalInput")
    wd_d = nc.dram_tensor("wd", [D, DFF], F32, kind="ExternalInput")
    qpos_d = nc.dram_tensor("qpos", [128, SLOTS], F32, kind="ExternalInput")
    thr_d = nc.dram_tensor("thr", [128, 1], F32, kind="ExternalInput")
    out_d = nc.dram_tensor("out", [NTOK, D], F32, kind="ExternalOutput")
    hscr = nc.dram_tensor("hscr", [NTOK, DFF], F32, kind="Internal")
    x1scr = nc.dram_tensor("x1scr", [NTOK, D], F32, kind="Internal")
    x1raw = nc.dram_tensor("x1raw", [NTOK, D], F32, kind="Internal")
    xkv_bf = nc.dram_tensor("xkv_bf", [S, D], BF16, kind="Internal")
    xq_bf = nc.dram_tensor("xq_bf", [NTOK, D], BF16, kind="Internal")
    wq_bf = nc.dram_tensor("wq_bf", [D, D], BF16, kind="Internal")
    wk_bf = nc.dram_tensor("wk_bf", [KV, D], BF16, kind="Internal")
    wv_bf = nc.dram_tensor("wv_bf", [KV, D], BF16, kind="Internal")
    wo_bf = nc.dram_tensor("wo_bf", [D, D], BF16, kind="Internal")
    wg_q = nc.dram_tensor("wg_q", [DFF, D], BF16, kind="Internal")
    wu_q = nc.dram_tensor("wu_q", [DFF, D], BF16, kind="Internal")
    wd_q = nc.dram_tensor("wd_q", [D, DFF], BF16, kind="Internal")
    dbg_d = {}
    for nm, shp in dbg:
        dbg_d[nm] = nc.dram_tensor("dbg_" + nm, list(shp), F32, kind="ExternalOutput")

    names = ["dSy", "dSc", "dCa", "dOut", "tXT", "tMM", "tPT", "tAV", "tMS",
             "vCh", "vMi", "vRp", "sCh", "sEv", "sEx", "gCh", "dBias", "dIn",
             "dDbg", "cXK", "cXQ", "cWKV", "cWQ", "cWO", "dTW"]
    SM = {n: Sem(nc, st, n) for n in names}

    def spair(nm):
        return [Sem(nc, st, nm + "0"), Sem(nc, st, nm + "1")]

    SP = {nm: spair(nm) for nm in
          ("xtg", "wst", "wst4", "xq32", "hwr", "hst6",
           "wd6", "x1sp", "x1rd", "outw", "x1rw", "x1fr")}
    SP["mst"] = [Sem(nc, st, "mst%d" % q) for q in range(4)]
    SP["w5st"] = [Sem(nc, st, "w5q%d" % q) for q in range(6)]
    SP["tf"] = [Sem(nc, st, "tf%d" % q) for q in range(4)]
    SP["tfg"] = [Sem(nc, st, "tfg%d" % q) for q in range(4)]
    SP["tfw"] = [Sem(nc, st, "tfw%d" % q) for q in range(2)]

    def dump(name, ap_fn):
        if name not in dbg_d:
            return
        with nc.Block() as B:
            @B.gpsimd
            def _(g):
                SM["dDbg"].inc(g.dma_start(dbg_d[name].ap(), ap_fn()), 16)
                g.wait_ge(SM["dDbg"].h, SM["dDbg"].n)

    ident = st.enter_context(nc.sbuf_tensor("ident", [128, 128], BF16))
    pcol = st.enter_context(nc.sbuf_tensor("pcol", [128, 1], F32))
    invf = st.enter_context(nc.sbuf_tensor("invf", [128, HJ], F32))
    maskw = st.enter_context(nc.sbuf_tensor("maskw", [128, 512], BF16))
    thr_s = st.enter_context(nc.sbuf_tensor("thr_s", [128, 1], F32))
    qpos_s = st.enter_context(nc.sbuf_tensor("qpos_s", [128, SLOTS], F32))
    aux_st = ExitStack()
    ktab_c = aux_st.enter_context(nc.sbuf_tensor("ktab_c", [128, NB * HJ], BF16, side="right"))
    ktab_s = aux_st.enter_context(nc.sbuf_tensor("ktab_s", [128, NB * HJ], BF16, side="right"))
    qtab_c = aux_st.enter_context(nc.sbuf_tensor("qtab_c", [128, SLOTS * HJ], BF16, side="right"))
    qtab_s = aux_st.enter_context(nc.sbuf_tensor("qtab_s", [128, SLOTS * HJ], BF16, side="right"))
    bias_bf = st.enter_context(nc.sbuf_tensor("bias_bf", [1, 2 * D + 2 * KV], BF16))
    ones_bf = st.enter_context(nc.sbuf_tensor("ones_bf", [1, 128], BF16))
    onescol_bf = st.enter_context(nc.sbuf_tensor("onescol_bf", [128, 1], BF16))
    onescol = st.enter_context(nc.sbuf_tensor("onescol", [128, 1], F32))
    onesrow = st.enter_context(nc.sbuf_tensor("onesrow", [1, 128], F32))
    qnsq = st.enter_context(nc.sbuf_tensor("qnsq", [128, NUNITS], F32))
    negM = st.enter_context(nc.sbuf_tensor("negM", [128, NUNITS], F32))
    kml2 = st.enter_context(nc.sbuf_tensor("kml2", [128, NKV], F32))
    krow = st.enter_context(nc.sbuf_tensor("krow", [1, 16], F32))
    krow_bf = st.enter_context(nc.sbuf_tensor("krow_bf", [1, 16], BF16))
    scl = {}
    for nm in ("s_x", "r_x", "s_h", "r_h", "sgx", "sux", "shd"):
        scl[nm] = st.enter_context(nc.sbuf_tensor(nm, [128, SLOTS], F32))
    wsc = st.enter_context(nc.sbuf_tensor("wsc", [128, 16], F32))
    stat = st.enter_context(nc.sbuf_tensor("stat", [128, 64], F32))
    stat2 = st.enter_context(nc.sbuf_tensor("stat2", [128, 64], F32))
    mean_tiles = []
    mt_bounds = []
    for wten_, rows_, colw_ in ((wg_d, DFF, D), (wu_d, DFF, D), (wd_d, D, DFF)):
        cnt0 = len(mean_tiles)
        for r0 in range(0, rows_, 128):
            for c0 in range(0, colw_, 1024):
                mean_tiles.append((wten_, r0, c0, min(1024, colw_ - c0)))
        mt_bounds.append(len(mean_tiles) - cnt0)
    n_mt = len(mean_tiles)
    asum = aux_st.enter_context(nc.sbuf_tensor("asum", [128, n_mt], F32, side="right"))
    mst_t = aux_st.enter_context(nc.sbuf_tensor("mstage", [128, 4 * 1024], BF16, side="right"))

    def mstage(p):
        return mst_t[:, (p % 4) * 1024:((p % 4) + 1) * 1024]

    junk_bf = st.enter_context(nc.sbuf_tensor("junk_bf", [128, max(D, 2048)], BF16))
    t11 = st.enter_context(nc.sbuf_tensor("t11", [1, 16], F32))
    posc = st.enter_context(nc.sbuf_tensor("posc", [128, 1], F32))
    tabt = aux_st.enter_context(nc.sbuf_tensor("tabt", [128, 4 * HJ], F32, side="right"))
    tabti = aux_st.enter_context(nc.sbuf_tensor("tabti", [128, HJ], I32, side="right"))
    p0_st = ExitStack()
    iot512 = p0_st.enter_context(nc.sbuf_tensor("iot512", [128, 512], F32, side="right"))


    with nc.Block() as B:
        @B.gpsimd
        def _(g):
            g.iota(iot512[:, :], [[1, 512]], channel_multiplier=0,
                   allow_small_or_imprecise_dtypes=True)
            g.iota(pcol[:, :], [[0, 1]], channel_multiplier=1,
                   allow_small_or_imprecise_dtypes=True)
            g.iota(invf[:, :], [[1, HJ]], channel_multiplier=0,
                   allow_small_or_imprecise_dtypes=True)
            SM["gCh"].inc(g.memset(ident[:, :], 0.0))
            g.wait_ge(SM["gCh"].h, SM["gCh"].n)
            g.affine_select(out=ident[:, :], in_=ident[:, :],
                            compare_op=OP.not_equal, fill=1.0,
                            base=0, pattern=[[-1, 128]], channel_multiplier=1)
            g.memset(ones_bf[:, :], 1.0)
            g.memset(onescol_bf[:, :], 1.0)
            g.memset(onescol[:, :], 1.0)
            g.memset(onesrow[:, :], 1.0)
            g.memset(stat[:, :], 0.0)
            g.memset(stat2[:, :], 0.0)
            g.memset(t11[:, :], 0.0)
            g.memset(krow[:, :], 0.0)
            g.memset(krow_bf[:, :], 0.0)
            g.memset(wsc[:, :], 0.0)
            g.memset(junk_bf[:, :], 0.0)
            g.memset(qnsq[:, :], 0.0)
            for nm in scl:
                g.memset(scl[nm][:, :], 1.0)
            for dst, src in ((bias_bf[:, 0:D], bq_d), (bias_bf[:, D:2 * D], bo_d),
                             (bias_bf[:, 2 * D:2 * D + KV], bk_d),
                             (bias_bf[:, 2 * D + KV:2 * D + 2 * KV], bv_d)):
                SM["dBias"].inc(g.dma_start(dst, src.ap()), 16)
            SM["cWKV"].inc(g.dma_start(wk_bf.ap(), wk_d.ap()), 16)
            SM["cWKV"].inc(g.dma_start(wv_bf.ap(), wv_d.ap()), 16)
            for q in range(4):
                SM["cXK"].inc(g.dma_start(
                    xkv_bf.ap()[q * (S // 4):(q + 1) * (S // 4), :],
                    x_kv.ap()[q * (S // 4):(q + 1) * (S // 4), :]), 16)
            SM["cXQ"].inc(g.dma_start(xq_bf.ap(), x_q.ap()), 16)
            for q in range(2):
                SM["cWQ"].inc(g.dma_start(
                    wq_bf.ap()[q * (D // 2):(q + 1) * (D // 2), :],
                    wq_d.ap()[q * (D // 2):(q + 1) * (D // 2), :]), 16)
                SM["cWO"].inc(g.dma_start(
                    wo_bf.ap()[q * (D // 2):(q + 1) * (D // 2), :],
                    wo_d.ap()[q * (D // 2):(q + 1) * (D // 2), :]), 16)

        @B.sync
        def _(s):
            SM["dIn"].inc(s.dma_start(thr_s[:, :], thr_d.ap()), 16)
            SM["dIn"].inc(s.dma_start(qpos_s[:, :], qpos_d.ap()), 16)

    with nc.Block() as B:
        @B.vector
        def _(v):
            v.wait_ge(SM["dIn"].h, SM["dIn"].n)
            v.tensor_scalar(maskw[:, :], iot512[:, :], thr_s[:, :], None, OP.is_lt)

        @B.scalar
        def _(sc):
            SM["sEv"].inc(sc.activation(invf[:, :], invf[:, :], AF.Exp,
                                        bias=0.0, scale=-math.log(ROPE_BASE) / HJ))

    def emit_table(B, pos_col, out_c, out_s, pre_scale, add_base):
        u = tabt[:, 0:HJ]
        f = tabt[:, HJ:2 * HJ]
        g2 = tabt[:, 2 * HJ:3 * HJ]
        arg = tabt[:, 3 * HJ:4 * HJ]

        @B.vector
        def _(v):
            if add_base is not None:
                SM["vCh"].inc(v.tensor_scalar(posc[:, :], pcol[:, :],
                                              float(add_base), None, OP.add))
                v.wait_ge(SM["vCh"].h, SM["vCh"].n)
                pc = posc[:, :]
            else:
                pc = pos_col
            SM["vCh"].inc(v.tensor_scalar(u, invf[:, :], pc, INV2PI, OP.mult, OP.mult))
            v.wait_ge(SM["vCh"].h, SM["vCh"].n)
            SM["vCh"].inc(v.tensor_copy(tabti[:, :], u))
            v.wait_ge(SM["vCh"].h, SM["vCh"].n)
            SM["vCh"].inc(v.tensor_copy(g2, tabti[:, :]))
            v.wait_ge(SM["vCh"].h, SM["vCh"].n)
            SM["vCh"].inc(v.tensor_tensor(f, u, g2, OP.subtract))
            v.wait_ge(SM["vCh"].h, SM["vCh"].n)
            SM["vCh"].inc(v.tensor_scalar(g2, f, 0.5, None, OP.is_ge))
            v.wait_ge(SM["vCh"].h, SM["vCh"].n)
            SM["vCh"].inc(v.tensor_tensor(arg, g2, f, OP.subtract))

        @B.scalar
        def _(sc):
            sc.wait_ge(SM["vCh"].h, SM["vCh"].n)
            SM["sEx"].inc(sc.activation(out_s, arg, AF.Sin,
                                        bias=0.0, scale=-2.0 * math.pi))

        @B.vector
        def _(v):
            v.wait_ge(SM["sEx"].h, SM["sEx"].n)
            SM["vCh"].inc(v.tensor_scalar(u, u, 0.25, None, OP.add))
            v.wait_ge(SM["vCh"].h, SM["vCh"].n)
            SM["vCh"].inc(v.tensor_copy(tabti[:, :], u))
            v.wait_ge(SM["vCh"].h, SM["vCh"].n)
            SM["vCh"].inc(v.tensor_copy(g2, tabti[:, :]))
            v.wait_ge(SM["vCh"].h, SM["vCh"].n)
            SM["vCh"].inc(v.tensor_tensor(f, u, g2, OP.subtract))
            v.wait_ge(SM["vCh"].h, SM["vCh"].n)
            SM["vCh"].inc(v.tensor_scalar(g2, f, 0.5, None, OP.is_ge))
            v.wait_ge(SM["vCh"].h, SM["vCh"].n)
            SM["vCh"].inc(v.tensor_tensor(arg, g2, f, OP.subtract))

        @B.scalar
        def _(sc):
            sc.wait_ge(SM["vCh"].h, SM["vCh"].n)
            SM["sEx"].inc(sc.activation(out_c, arg, AF.Sin,
                                        bias=0.0, scale=-2.0 * math.pi))

        @B.vector
        def _(v):
            v.wait_ge(SM["sEx"].h, SM["sEx"].n)
            if pre_scale != 1.0:
                v.tensor_scalar(out_s, out_s, float(pre_scale), None, OP.mult)
                SM["vCh"].inc(v.tensor_scalar(out_c, out_c, float(pre_scale),
                                              None, OP.mult))
                v.wait_ge(SM["vCh"].h, SM["vCh"].n)

    p0_st.close()

    bank_free = {}

    def emit_ct_tile(B, src_ap, width, stage_buf, tp_banks, dest_ap, war,
                     dest_war=None, stsem=None, evac="scalar"):
        nch = width // 128

        @B.gpsimd
        def _(g):
            if war is not None:
                g.wait_ge(war[0].h, war[1])
            stsem.inc(g.dma_start(stage_buf, src_ap), 16)
        cast_n = stsem.n
        grp_t = []

        @B.tensor
        def _(te):
            te.wait_ge(stsem.h, cast_n)
            for j in range(nch):
                bank = tp_banks[(j // 8) % 2]
                if j % 8 == 0:
                    te.wait_ge(SM["sEv"].h, bank_free.get(id(bank), 0))
                ins = te.transpose(bank[:, (j % 8) * 128:(j % 8 + 1) * 128],
                                   stage_buf[:, j * 128:(j + 1) * 128], ident[:, :])
                if j % 8 == 7 or j == nch - 1:
                    grp_t.append(SM["tXT"].inc(ins))

        def evac_body(e):
            if dest_war is not None:
                e.wait_ge(dest_war[0].h, dest_war[1])
            for gi in range((nch + 7) // 8):
                bank = tp_banks[gi % 2]
                e.wait_ge(SM["tXT"].h, grp_t[gi])
                lo = gi * 8
                hi = min(lo + 8, nch)
                if evac == "scalar":
                    SM["sEv"].inc(e.activation(dest_ap(lo, hi),
                                               bank[:, 0:(hi - lo) * 128],
                                               AF.Copy, bias=0.0, scale=1.0))
                else:
                    SM["sEv"].inc(e.tensor_copy(dest_ap(lo, hi),
                                                bank[:, 0:(hi - lo) * 128]))
                bank_free[id(bank)] = SM["sEv"].n
        if evac == "scalar":
            B.scalar(evac_body)
        else:
            B.vector(evac_body)
        return SM["sEv"].n

    attn_st = ExitStack()
    qf = attn_st.enter_context(nc.sbuf_tensor("qf", [128, NUNITS * 128], BF16))
    p13 = ExitStack()
    kf = p13.enter_context(nc.sbuf_tensor("kf", [128, NKV * S], BF16))
    vsb = p13.enter_context(nc.sbuf_tensor("vsb", [128, NB * KV], BF16))

    mt_p1 = {}
    mt_p2 = {}
    n_hosts = NB + NOC * SLOTS
    per_host = (n_mt + n_hosts - 1) // n_hosts
    assert per_host <= 8
    for t in range(n_mt):
        host = min(t // per_host, n_hosts - 1)
        if host < NB:
            mt_p1.setdefault(host, []).append(t)
        else:
            mt_p2.setdefault(host - NB, []).append(t)
    mt_dma, mt_red = {}, {}

    def emit_mean(B, tiles):
        for t in tiles:
            wten, r0, c0, cw = mean_tiles[t]

            @B.gpsimd
            def _(g, t=t, wten=wten, r0=r0, c0=c0, cw=cw):
                if t >= 4:
                    g.wait_ge(SM["vMi"].h, mt_red[t - 4])
                mt_dma[t] = SP["mst"][t % 4].inc(g.dma_start(
                    mstage(t)[:, 0:cw], wten.ap()[r0:r0 + 128, c0:c0 + cw]), 16)

            @B.vector
            def _(v, t=t, cw=cw):
                v.wait_ge(SP["mst"][t % 4].h, mt_dma[t])
                mt_red[t] = SM["vMi"].inc(v.reduce_sum(
                    asum[:, t:t + 1], mstage(t)[:, 0:cw], AX.X,
                    apply_absolute_value=True))

    if upto >= 1:
        p1 = ExitStack()
        wkT = p1.enter_context(nc.sbuf_tensor("wkT", [128, DC * KV], BF16))
        wvT = p1.enter_context(nc.sbuf_tensor("wvT", [128, DC * KV], BF16))
        xtg = _dbuf(p1, nc, "xtg", DC * 256, BF16)
        krope = _dbuf(p1, nc, "krope", KV, BF16)
        ropet = p1.enter_context(nc.sbuf_tensor("ropet", [128, 4 * NKV * HJ], F32))
        p1p = ExitStack()
        kps = p1p.enter_context(nc.psum_tensor("kps", [128, 1024], F32))
        vps = p1p.enter_context(nc.psum_tensor("vps", [128, 1024], F32))
        tkp = p1p.enter_context(nc.psum_tensor("tkp", [128, 2048], BF16))

        with nc.Block() as B:
            @B.sync
            def _(s):
                s.wait_ge(SM["cWKV"].h, SM["cWKV"].n)
                for wbf, wT in ((wk_bf, wkT), (wv_bf, wvT)):
                    for j in range(DC):
                        SM["dTW"].inc(s.dma_start(
                            wT[:, j * KV:(j + 1) * KV],
                            wbf.ap()[0:KV, j * 128:(j + 1) * 128],
                            transpose=True), 16)
            twv_tot = SM["dTW"].n

            blk_kmm, blk_vmm, blk_rope, blk_ktr = [], [], [], []
            blk_kf_ev, blk_v_ev = [], []
            xtg_marks = []
            for g4 in range(NB // 2):
                @B.sync
                def _(s, g4=g4):
                    if g4 == 0:
                        s.wait_ge(SM["cXK"].h, SM["cXK"].n)
                    if g4 >= 2:
                        s.wait_ge(SM["tMM"].h, blk_vmm[(g4 - 2) * 2 + 1])
                    for j in range(DC):
                        SP["xtg"][g4 % 2].inc(s.dma_start(
                            xtg(g4)[:, j * 256:(j + 1) * 256],
                            xkv_bf.ap()[g4 * 256:(g4 + 1) * 256,
                                        j * 128:(j + 1) * 128],
                            transpose=True), 16)
                xtg_marks.append(SP["xtg"][g4 % 2].n)

                for kb in range(g4 * 2, g4 * 2 + 2):
                    emit_table(B, None, ktab_c[:, kb * HJ:(kb + 1) * HJ],
                               ktab_s[:, kb * HJ:(kb + 1) * HJ], 1.0, 128 * kb)
                    emit_mean(B, mt_p1.get(kb, []))
                    half = (kb % 2) * 512
                    kslice = kps[:, half:half + KV]
                    vslice = vps[:, half:half + KV]

                    @B.tensor
                    def _(te, g4=g4, kb=kb, kslice=kslice, vslice=vslice):
                        if kb == 0:
                            te.wait_ge(SM["dBias"].h, SM["dBias"].n)
                            te.wait_ge(SM["dTW"].h, twv_tot)
                        if kb % 2 == 0:
                            te.wait_ge(SP["xtg"][g4 % 2].h, xtg_marks[g4])
                        if kb >= 2:
                            te.wait_ge(SM["vRp"].h, blk_rope[kb - 2])
                            te.wait_ge(SM["sEv"].h, blk_v_ev[kb - 2])
                        for j in range(DC):
                            lhs = xtg(g4)[:, j * 256 + (kb % 2) * 128:
                                          j * 256 + (kb % 2 + 1) * 128]
                            te.matmul(kslice, lhs, wkT[:, j * KV:(j + 1) * KV],
                                      start=(j == 0), stop=False)
                            te.matmul(vslice, lhs, wvT[:, j * KV:(j + 1) * KV],
                                      start=(j == 0), stop=False)
                        blk_kmm.append(SM["tMM"].inc(te.matmul(
                            kslice, ones_bf[0:1, :],
                            bias_bf[0:1, 2 * D:2 * D + KV],
                            start=False, stop=True)))
                        blk_vmm.append(SM["tMM"].inc(te.matmul(
                            vslice, ones_bf[0:1, :],
                            bias_bf[0:1, 2 * D + KV:2 * D + 2 * KV],
                            start=False, stop=True)))

                    @B.vector
                    def _(v, kb=kb, kslice=kslice):
                        v.wait_ge(SM["tMM"].h, blk_kmm[kb])
                        if kb >= 2:
                            v.wait_ge(SM["tPT"].h, blk_ktr[kb - 2])
                        kp = kslice.rearrange("p (h j two) -> p h j two",
                                              h=NKV, two=2)
                        ke, ko = kp[:, :, :, 0], kp[:, :, :, 1]
                        cb = ktab_c[:, kb * HJ:(kb + 1) * HJ] \
                            .rearrange("p (o j) -> p o j", o=1) \
                            .broadcast_to((128, NKV, HJ))
                        sb = ktab_s[:, kb * HJ:(kb + 1) * HJ] \
                            .rearrange("p (o j) -> p o j", o=1) \
                            .broadcast_to((128, NKV, HJ))
                        n4 = NKV * HJ

                        def rt(q):
                            return ropet[:, q * n4:(q + 1) * n4] \
                                .rearrange("p (h j) -> p h j", h=NKV)
                        v.tensor_tensor(rt(0), ke, cb, OP.mult)
                        v.tensor_tensor(rt(1), ko, sb, OP.mult)
                        v.tensor_tensor(rt(2), ko, cb, OP.mult)
                        SM["vCh"].inc(v.tensor_tensor(rt(3), ke, sb, OP.mult))
                        v.wait_ge(SM["vCh"].h, SM["vCh"].n)
                        kr = krope(kb).rearrange("p (h j two) -> p h j two",
                                                 h=NKV, two=2)
                        v.tensor_tensor(kr[:, :, :, 0], rt(0), rt(1), OP.subtract)
                        blk_rope.append(SM["vRp"].inc(v.tensor_tensor(
                            kr[:, :, :, 1], rt(2), rt(3), OP.add)))

                    @B.tensor
                    def _(te, kb=kb):
                        te.wait_ge(SM["vRp"].h, blk_rope[kb])
                        if kb >= 2:
                            te.wait_ge(SM["sEv"].h, blk_kf_ev[kb - 2])
                        toff = (kb % 2) * 1024
                        last = None
                        for h4 in range(NKV):
                            last = te.transpose(
                                tkp[:, toff + h4 * 128: toff + (h4 + 1) * 128],
                                krope(kb)[:, h4 * HD:(h4 + 1) * HD], ident[:, :])
                        blk_ktr.append(SM["tPT"].inc(last))

                    @B.scalar
                    def _(sc, kb=kb):
                        sc.wait_ge(SM["tPT"].h, blk_ktr[kb])
                        toff = (kb % 2) * 1024
                        kdst = kf[:].rearrange("p (h s) -> p h s", h=NKV) \
                            [:, :, kb * 128:(kb + 1) * 128]
                        ksrc = tkp[:, toff:toff + NKV * 128] \
                            .rearrange("p (h t) -> p h t", h=NKV)
                        blk_kf_ev.append(SM["sEv"].inc(sc.activation(
                            kdst, ksrc, AF.Copy, bias=0.0, scale=1.0)))
                        sc.wait_ge(SM["tMM"].h, blk_vmm[kb])
                        blk_v_ev.append(SM["sEv"].inc(sc.activation(
                            vsb[:, kb * KV:(kb + 1) * KV],
                            vps[:, (kb % 2) * 512:(kb % 2) * 512 + KV],
                            AF.Copy, bias=0.0, scale=1.0)))

        p1.close()
        p1b = ExitStack()
        kf2 = p1b.enter_context(nc.sbuf_tensor("kf2", [128, NKV * S], BF16))
        with nc.Block() as B:
            @B.scalar
            def _(sc):
                last = None
                for c0 in range(0, NKV * S, 2048):
                    last = sc.activation(kf2[:, c0:c0 + 2048], kf[:, c0:c0 + 2048],
                                         AF.Square, bias=0.0, scale=1.0)
                SM["sEv"].inc(last)
            sq_done = SM["sEv"].n
            red_marks = []
            for h4 in range(NKV):
                nchk = S // 512
                for p0 in range(0, nchk, 4):
                    phi = min(p0 + 4, nchk)

                    @B.tensor
                    def _(te, h4=h4, p0=p0, phi=phi):
                        te.wait_ge(SM["sEv"].h, sq_done)
                        if red_marks:
                            te.wait_ge(SM["vMi"].h, red_marks[-1])
                        last = None
                        for ch in range(p0, phi):
                            lc = ch - p0
                            bank = (kps, vps)[(lc // 2) % 2]
                            last = te.matmul(
                                bank[0:1, (lc % 2) * 512:(lc % 2) * 512 + 512],
                                onescol_bf[:, :],
                                kf2[:, h4 * S + ch * 512: h4 * S + (ch + 1) * 512],
                                start=True, stop=True)
                        SM["tMS"].inc(last)

                    @B.vector
                    def _(v, h4=h4, p0=p0, phi=phi):
                        v.wait_ge(SM["tMS"].h, SM["tMS"].n)
                        nlc = phi - p0
                        SM["vCh"].inc(v.reduce_max(t11[0:1, 8:9],
                                                   kps[0:1, 0:min(nlc, 2) * 512], AX.X))
                        v.wait_ge(SM["vCh"].h, SM["vCh"].n)
                        if nlc > 2:
                            SM["vCh"].inc(v.reduce_max(
                                t11[0:1, 9:10], vps[0:1, 0:(nlc - 2) * 512], AX.X))
                            v.wait_ge(SM["vCh"].h, SM["vCh"].n)
                            SM["vCh"].inc(v.tensor_tensor(t11[0:1, 8:9], t11[0:1, 8:9],
                                                          t11[0:1, 9:10], OP.max))
                            v.wait_ge(SM["vCh"].h, SM["vCh"].n)
                        SM["vCh"].inc(v.tensor_tensor(krow[0:1, h4:h4 + 1],
                                                      krow[0:1, h4:h4 + 1],
                                                      t11[0:1, 8:9], OP.max))
                        v.wait_ge(SM["vCh"].h, SM["vCh"].n)
                        red_marks.append(SM["vMi"].inc(v.nop()))

            @B.scalar
            def _(sc):
                sc.wait_ge(SM["vMi"].h, SM["vMi"].n)
                SM["sCh"].inc(sc.activation(krow[0:1, 8:8 + NKV], krow[0:1, 0:NKV],
                                            AF.Sqrt, bias=0.0, scale=1.0))
                sc.wait_ge(SM["sCh"].h, SM["sCh"].n)
                SM["sEv"].inc(sc.copy(krow_bf[0:1, 0:NKV], krow[0:1, 8:8 + NKV]))

            @B.tensor
            def _(te):
                te.wait_ge(SM["sEv"].h, SM["sEv"].n)
                SM["tMS"].inc(te.matmul(kps[:, 0:NKV], ones_bf[0:1, :],
                                        krow_bf[0:1, 0:NKV], start=True, stop=True))

            @B.scalar
            def _(sc):
                sc.wait_ge(SM["tMS"].h, SM["tMS"].n)
                SM["sEv"].inc(sc.activation(kml2[:, :], kps[:, 0:NKV], AF.Copy,
                                            bias=0.0, scale=1.0))
        p1b.close()
        p1p.close()
        dump("kf", lambda: kf[:, :])
        dump("vsb", lambda: vsb[:, :])
        dump("kml2", lambda: kml2[:, :])

    if upto >= 2:
        p2 = ExitStack()
        xqT = p2.enter_context(nc.sbuf_tensor("xqT", [128, SLOTS * D], BF16))
        wT2 = _dbuf(p2, nc, "wT2", DC * 512, BF16)
        qrope = _dbuf(p2, nc, "qrope", 512, BF16)
        qropet = p2.enter_context(nc.sbuf_tensor("qropet", [128, 4 * HPO * HJ], F32))
        p2p = ExitStack()
        qps = p2p.enter_context(nc.psum_tensor("qps", [128, 1024], F32))
        tq2 = p2p.enter_context(nc.psum_tensor("tq2", [128, 2048], BF16))

        with nc.Block() as B:
            @B.sync
            def _(s):
                s.wait_ge(SM["cXQ"].h, SM["cXQ"].n)
                gsz = min(512, NTOK)
                for r0 in range(0, NTOK, gsz):
                    for j in range(DC):
                        SM["dTW"].inc(s.dma_start(
                            xqT[:, j * NTOK + r0: j * NTOK + r0 + gsz],
                            xq_bf.ap()[r0:r0 + gsz, j * 128:(j + 1) * 128],
                            transpose=True), 16)
            xqt_tot = SM["dTW"].n

            q_mm, q_rope, q_tr, q_ev, q_sq = [], [], [], [], []
            wt2_marks = []
            for oc in range(NOC):
                @B.sync
                def _(s, oc=oc):
                    if oc == 0:
                        s.wait_ge(SM["cWQ"].h, SM["cWQ"].n)
                    if oc >= 2:
                        s.wait_ge(SM["tMM"].h, q_mm[-1])
                    for j in range(DC):
                        SP["wst"][oc % 2].inc(s.dma_start(
                            wT2(oc)[:, j * 512:(j + 1) * 512],
                            wq_bf.ap()[oc * 512:(oc + 1) * 512,
                                       j * 128:(j + 1) * 128],
                            transpose=True), 16)
                wt2_marks.append(SP["wst"][oc % 2].n)

                for i in range(SLOTS):
                    if oc == 0:
                        emit_table(B, qpos_s[:, i:i + 1],
                                   qtab_c[:, i * HJ:(i + 1) * HJ],
                                   qtab_s[:, i * HJ:(i + 1) * HJ], scale_q, None)
                        emit_mean(B, mt_p2.get(i, []))
                    else:
                        emit_mean(B, mt_p2.get(oc * SLOTS + i, []))
                    u0 = i * NH + oc * HPO
                    n_q = len(q_mm)

                    @B.tensor
                    def _(te, oc=oc, i=i, n_q=n_q):
                        if n_q == 0:
                            te.wait_ge(SM["dTW"].h, xqt_tot)
                        if i == 0:
                            te.wait_ge(SP["wst"][oc % 2].h, wt2_marks[oc])
                        if n_q >= 2:
                            te.wait_ge(SM["vRp"].h, q_rope[n_q - 2])
                        sl = qps[:, (n_q % 2) * 512:(n_q % 2) * 512 + 512]
                        for j in range(DC):
                            te.matmul(sl, xqT[:, j * NTOK + i * 128:
                                              j * NTOK + (i + 1) * 128],
                                      wT2(oc)[:, j * 512:(j + 1) * 512],
                                      start=(j == 0), stop=False)
                        q_mm.append(SM["tMM"].inc(te.matmul(
                            sl, ones_bf[0:1, :],
                            bias_bf[0:1, oc * 512:(oc + 1) * 512],
                            start=False, stop=True)))

                    @B.vector
                    def _(v, oc=oc, i=i, n_q=n_q):
                        v.wait_ge(SM["tMM"].h, q_mm[n_q])
                        if n_q >= 2:
                            v.wait_ge(SM["tPT"].h, q_tr[n_q - 2])
                            v.wait_ge(SM["sEx"].h, q_sq[n_q - 2])
                        sl = qps[:, (n_q % 2) * 512:(n_q % 2) * 512 + 512]
                        kp = sl.rearrange("p (h j two) -> p h j two", h=HPO, two=2)
                        ke, ko = kp[:, :, :, 0], kp[:, :, :, 1]
                        cb = qtab_c[:, i * HJ:(i + 1) * HJ] \
                            .rearrange("p (o j) -> p o j", o=1) \
                            .broadcast_to((128, HPO, HJ))
                        sb = qtab_s[:, i * HJ:(i + 1) * HJ] \
                            .rearrange("p (o j) -> p o j", o=1) \
                            .broadcast_to((128, HPO, HJ))
                        n4 = HPO * HJ

                        def rt(q):
                            return qropet[:, q * n4:(q + 1) * n4] \
                                .rearrange("p (h j) -> p h j", h=HPO)
                        v.tensor_tensor(rt(0), ke, cb, OP.mult)
                        v.tensor_tensor(rt(1), ko, sb, OP.mult)
                        v.tensor_tensor(rt(2), ko, cb, OP.mult)
                        SM["vCh"].inc(v.tensor_tensor(rt(3), ke, sb, OP.mult))
                        v.wait_ge(SM["vCh"].h, SM["vCh"].n)
                        qr = qrope(n_q).rearrange("p (h j two) -> p h j two",
                                                  h=HPO, two=2)
                        v.tensor_tensor(qr[:, :, :, 0], rt(0), rt(1), OP.subtract)
                        q_rope.append(SM["vRp"].inc(v.tensor_tensor(
                            qr[:, :, :, 1], rt(2), rt(3), OP.add)))

                    @B.scalar
                    def _(sc, u0=u0, n_q=n_q):
                        sc.wait_ge(SM["vRp"].h, q_rope[n_q])
                        last = None
                        for h4 in range(HPO):
                            jsl = (h4 + HPO * (n_q % 4)) % 16
                            last = sc.activation(
                                junk_bf[:, jsl * HD:(jsl + 1) * HD],
                                qrope(n_q)[:, h4 * HD:(h4 + 1) * HD],
                                AF.Square, bias=0.0, scale=1.0,
                                accum_out=qnsq[:, u0 + h4:u0 + h4 + 1])
                        q_sq.append(SM["sEx"].inc(last))

                    @B.tensor
                    def _(te, n_q=n_q):
                        te.wait_ge(SM["vRp"].h, q_rope[n_q])
                        if n_q >= 2:
                            te.wait_ge(SM["sEv"].h, q_ev[n_q - 2])
                        toff = (n_q % 2) * 1024
                        last = None
                        for h4 in range(HPO):
                            last = te.transpose(
                                tq2[:, toff + h4 * 128: toff + (h4 + 1) * 128],
                                qrope(n_q)[:, h4 * HD:(h4 + 1) * HD], ident[:, :])
                        q_tr.append(SM["tPT"].inc(last))

                    @B.vector
                    def _(v, u0=u0, n_q=n_q):
                        v.wait_ge(SM["tPT"].h, q_tr[n_q])
                        toff = (n_q % 2) * 1024
                        q_ev.append(SM["sEv"].inc(v.tensor_copy(
                            qf[:, u0 * 128:(u0 + HPO) * 128],
                            tq2[:, toff:toff + HPO * 128])))

        p2c = ExitStack()
        ps2c = p2c.enter_context(nc.psum_tensor("ps2c", [128, 512], F32))
        with nc.Block() as B:
            @B.vector
            def _(v):
                lo = 0
                last = None
                for wi, cnt in enumerate(mt_bounds):
                    v.wait_ge(SM["vMi"].h, mt_red[lo + cnt - 1])
                    last = v.reduce_sum(stat2[:, 8 + wi:9 + wi],
                                        asum[:, lo:lo + cnt], AX.X)
                    lo += cnt
                SM["vMi"].inc(last)

            @B.tensor
            def _(te):
                te.wait_ge(SM["vMi"].h, SM["vMi"].n)
                last = None
                for wi in range(3):
                    last = te.matmul(ps2c[0:1, wi:wi + 1], stat2[:, 8 + wi:9 + wi],
                                     onescol[:, :], start=True, stop=True)
                SM["tMS"].inc(last)

            @B.scalar
            def _(sc):
                sc.wait_ge(SM["tMS"].h, SM["tMS"].n)
                SM["sEv"].inc(sc.activation(t11[0:1, 0:3], ps2c[0:1, 0:3], AF.Copy,
                                            bias=0.0, scale=1.0))

            @B.vector
            def _(v):
                v.wait_ge(SM["sEv"].h, SM["sEv"].n)
                SM["vMi"].inc(v.tensor_scalar(t11[0:1, 4:7], t11[0:1, 0:3],
                                              1.0 / (DFF * D), QEPS, OP.mult, OP.add))

            @B.tensor
            def _(te):
                te.wait_ge(SM["vMi"].h, SM["vMi"].n)
                last = None
                for wi in range(3):
                    last = te.matmul(ps2c[:, 8 + wi:9 + wi], onesrow[0:1, :],
                                     t11[0:1, 4 + wi:5 + wi], start=True, stop=True)
                SM["tMS"].inc(last)

            @B.scalar
            def _(sc):
                sc.wait_ge(SM["tMS"].h, SM["tMS"].n)
                SM["sEv"].inc(sc.activation(wsc[:, 0:3], ps2c[:, 8:11], AF.Copy,
                                            bias=0.0, scale=1.0))

            @B.vector
            def _(v):
                v.wait_ge(SM["sEv"].h, SM["sEv"].n)
                v.tensor_scalar(wsc[:, 3:6], wsc[:, 0:3], 0.5, None, OP.mult)
                SM["vCh"].inc(v.tensor_scalar(wsc[:, 6:9], wsc[:, 0:3], -0.5,
                                              None, OP.mult))
                v.wait_ge(SM["vCh"].h, SM["vCh"].n)
        p2c.close()

        with nc.Block() as B:
            @B.scalar
            def _(sc):
                SM["sEv"].inc(sc.activation(negM[:, :], qnsq[:, :], AF.Sqrt,
                                            bias=0.0, scale=1.0))

            @B.vector
            def _(v):
                v.wait_ge(SM["sEv"].h, SM["sEv"].n)
                qn4 = negM[:, :].rearrange("p (i k r) -> p i k r", i=SLOTS, k=NKV)
                km4 = kml2[:, :].rearrange("p (a k b) -> p a k b", a=1, b=1) \
                    .broadcast_to((128, SLOTS, NKV, REP))
                SM["vCh"].inc(v.tensor_tensor(qn4, qn4, km4, OP.mult))
                v.wait_ge(SM["vCh"].h, SM["vCh"].n)
                SM["vCh"].inc(v.tensor_scalar(negM[:, :], negM[:, :], -1.02, -1.0,
                                              OP.mult, OP.add))
                v.wait_ge(SM["vCh"].h, SM["vCh"].n)
        p2p.close()
        p2.close()
        dump("qf", lambda: qf[:, :])
        dump("negM", lambda: negM[:, :])

    fac_tiles = []
    WGW = min(D, 2048)
    for rb in range(DFF // 128):
        fac_tiles.append((wg_d, wg_q, rb * 128, 0, WGW, 3))
    for rb in range(DFF // 128):
        fac_tiles.append((wu_d, wu_q, rb * 128, 0, WGW, 4))
    n_fac_gu = len(fac_tiles)
    WCW = min(2048, DFF)
    for rb in range(D // 128):
        for cc in range(DFF // WCW):
            fac_tiles.append((wd_d, wd_q, rb * 128, cc * WCW, WCW, 5))
    n_fac = len(fac_tiles)
    fac_ld, fac_mark, fac_wb = {}, {}, {}
    FAC = {}

    def fac_alloc(stack, tag, nd=4, swdge=False):
        FAC["wfst"] = _dbuf(stack, nc, "wfst" + tag, 2048, F32, n=nd)
        FAC["nd"] = nd
        FAC["swdge"] = swdge
        FAC["tbuf5"] = _dbuf(stack, nc, "tbuf5" + tag, 2048, BF16, n=2)
        FAC["pnb3"] = stack.enter_context(
            nc.sbuf_tensor("pnb3" + tag, [128, 2048], BF16))

    def emit_tern(B, lo, hi):
        wfst, tbuf5, pnb3 = FAC["wfst"], FAC["tbuf5"], FAC["pnb3"]
        nd = FAC["nd"]
        ld_sems = SP["tfg"] if FAC["swdge"] else SP["tf"]
        for t in range(lo, hi):
            wten, wscr, r0, c0, cw, ti = fac_tiles[t]
            if FAC["swdge"]:
                dec = B.gpsimd
            else:
                dec = B.sync if t % 2 == 0 else B.scalar

            @dec
            def _(eng, t=t, wten=wten, r0=r0, c0=c0, cw=cw, nd=nd):
                if t >= nd:
                    eng.wait_ge(SM["vMi"].h, fac_mark[t - nd])
                fac_ld[t] = ld_sems[t % 4].inc(eng.dma_start(
                    wfst(t)[:, 0:cw], wten.ap()[r0:r0 + 128, c0:c0 + cw]), 16)

            @B.vector
            def _(v, t=t, cw=cw, ti=ti, ld_sems=ld_sems):
                v.wait_ge(ld_sems[t % 4].h, fac_ld[t])
                if t >= 1:
                    v.wait_ge(SM["vMi"].h, fac_mark[t - 1])
                if t >= 2:
                    v.wait_ge(SP["tfw"][t % 2].h, fac_wb[t - 2])
                SM["vCh"].inc(v.tensor_scalar(
                    pnb3[:, 0:cw], wfst(t)[:, 0:cw],
                    wsc[:, ti + 3:ti + 4], -1.0, OP.is_le, OP.mult))
                v.wait_ge(SM["vCh"].h, SM["vCh"].n)
                fac_mark[t] = SM["vMi"].inc(v.scalar_tensor_tensor(
                    tbuf5(t)[:, 0:cw], wfst(t)[:, 0:cw], wsc[:, ti:ti + 1],
                    pnb3[:, 0:cw], OP.is_ge, OP.add))

            @B.gpsimd
            def _(g, t=t, wscr=wscr, r0=r0, c0=c0, cw=cw):
                g.wait_ge(SM["vMi"].h, fac_mark[t])
                fac_wb[t] = SP["tfw"][t % 2].inc(g.dma_start(
                    wscr.ap()[r0:r0 + 128, c0:c0 + cw], tbuf5(t)[:, 0:cw]), 16)

    if upto >= 3:
        p3 = ExitStack()
        fac_alloc(p3, "a", nd=3)
        psb = _dbuf(p3, nc, "psb", maxL, BF16)
        pnb = _dbuf(p3, nc, "pnb", maxL, BF16)
        ptb = _dbuf(p3, nc, "ptb", 512, BF16)
        p3p = ExitStack()
        sps = p3p.enter_context(nc.psum_tensor("sps", [128, 2048], F32))
        ptp = p3p.enter_context(nc.psum_tensor("ptp", [128, 2048], BF16))
        avp = p3p.enter_context(nc.psum_tensor("avp", [128, 1024], F32))

        with nc.Block() as B:
            u_exp, u_norm, u_lastptr, u_av_last, u_ave = [], [], [], [], []
            chunk_exp = []
            p3_sEx_base = SM["sEx"].n
            av_marks = []
            pte_marks = []
            gchunk = [0]
            ggrp = [0]
            unit_info = []

            def emit_ptav(uidx):
                u, i, nch, L, kvh = unit_info[uidx]
                for grp in range(nch):
                    gg = ggrp[0]
                    gpar = gg % 2

                    @B.tensor
                    def _(te, u=u, grp=grp, gpar=gpar, gg=gg):
                        if grp == 0:
                            te.wait_ge(SM["vMi"].h, u_norm[u])
                            if u >= 2:
                                te.wait_ge(SM["vMi"].h, u_ave[u - 2])
                        if gg >= 2:
                            te.wait_ge(SM["vMi"].h, pte_marks[gg - 2])
                        last = None
                        for b in range(4):
                            last = te.transpose(
                                ptp[:, gpar * 1024 + b * 128:
                                    gpar * 1024 + (b + 1) * 128],
                                pnb(u)[:, (grp * 4 + b) * 128:
                                           (grp * 4 + b + 1) * 128],
                                ident[:, :])
                        SM["tPT"].inc(last)
                    ptr_mark = SM["tPT"].n

                    @B.vector
                    def _(v, gpar=gpar, gg=gg, ptr_mark=ptr_mark):
                        v.wait_ge(SM["tPT"].h, ptr_mark)
                        if gg >= 2:
                            v.wait_ge(SM["tAV"].h, av_marks[gg - 2])
                        pte_marks.append(SM["vMi"].inc(v.tensor_copy(
                            ptb(gpar)[:, :],
                            ptp[:, gpar * 1024:gpar * 1024 + 512])))

                    @B.tensor
                    def _(te, u=u, grp=grp, nch=nch, kvh=kvh, gpar=gpar, gg=gg):
                        te.wait_ge(SM["vMi"].h, pte_marks[gg])
                        last = None
                        for b in range(4):
                            kb = grp * 4 + b
                            last = te.matmul(
                                avp[:, (u % 2) * 512:(u % 2) * 512 + 128],
                                vsb[:, kb * KV + kvh * HD: kb * KV + kvh * HD + HD],
                                ptb(gpar)[:, b * 128:(b + 1) * 128],
                                start=(kb == 0), stop=(kb == nch * 4 - 1))
                        av_marks.append(SM["tAV"].inc(last))
                    ggrp[0] += 1
                u_lastptr.append(SM["tPT"].n)
                u_av_last.append(SM["tAV"].n)

                @B.vector
                def _(v, u=u, uidx=uidx):
                    v.wait_ge(SM["tAV"].h, u_av_last[uidx])
                    u_ave.append(SM["vMi"].inc(v.tensor_copy(
                        qf[:, u * 128:(u + 1) * 128],
                        avp[:, (u % 2) * 512:(u % 2) * 512 + 128])))

            for i in range(SLOTS):
                L = SLENS[i] * 128
                nch = SLENS[i] // 4
                for h in range(NH):
                    u = i * NH + h
                    kvh = h // REP
                    unit_info.append((u, i, nch, L, kvh))

                    @B.tensor
                    def _(te, u=u, i=i, nch=nch, kvh=kvh):
                        for cch in range(nch):
                            gc = gchunk[0]
                            if gc >= 4:
                                te.wait_ge(SM["sEx"].h, p3_sEx_base + gc - 3)
                            SM["tMM"].inc(te.matmul(
                                sps[:, (gc % 4) * 512:(gc % 4) * 512 + 512],
                                qf[:, u * 128:(u + 1) * 128],
                                kf[:, kvh * S + cch * 512: kvh * S + (cch + 1) * 512],
                                start=True, stop=True))
                            gchunk[0] += 1
                    mm_base = SM["tMM"].n - nch

                    @B.scalar
                    def _(sc, u=u, i=i, nch=nch, mm_base=mm_base):
                        if u >= 2:
                            sc.wait_ge(SM["tPT"].h, u_lastptr[u - 2])
                            sc.wait_ge(SM["vMi"].h, u_norm[u - 2])
                        for cch in range(nch):
                            gc = gchunk[0] - nch + cch
                            sc.wait_ge(SM["tMM"].h, mm_base + cch + 1)
                            dst = psb if cch < nch - 1 else pnb
                            acc = stat[:, (u % 2) * 16 + cch:(u % 2) * 16 + cch + 1]
                            chunk_exp.append(SM["sEx"].inc(sc.activation(
                                dst(u)[:, cch * 512:(cch + 1) * 512],
                                sps[:, (gc % 4) * 512:(gc % 4) * 512 + 512],
                                AF.Exp, bias=negM[:, u:u + 1], scale=1.0,
                                accum_out=acc)))
                        u_exp.append(SM["sEx"].n)

                    if u >= 1:
                        emit_ptav(u - 1)

                    @B.vector
                    def _(v, u=u, i=i, nch=nch, L=L):
                        v.wait_ge(SM["sEx"].h, u_exp[u])
                        b16 = (u % 2) * 16
                        w0 = (nch - 1) * 512
                        SM["vCh"].inc(v.tensor_tensor(
                            psb(u)[:, w0:w0 + 512],
                            pnb(u)[:, w0:w0 + 512], maskw[:, :], OP.mult))
                        v.wait_ge(SM["vCh"].h, SM["vCh"].n)
                        SM["vCh"].inc(v.reduce_sum(
                            stat[:, b16 + nch - 1:b16 + nch],
                            psb(u)[:, w0:w0 + 512], AX.X))
                        v.wait_ge(SM["vCh"].h, SM["vCh"].n)
                        SM["vCh"].inc(v.reduce_sum(stat2[:, u % 2:u % 2 + 1],
                                                   stat[:, b16:b16 + nch], AX.X))
                        v.wait_ge(SM["vCh"].h, SM["vCh"].n)
                        SM["vCh"].inc(v.reciprocal(stat2[:, 2 + u % 2:3 + u % 2],
                                                   stat2[:, u % 2:u % 2 + 1]))
                        v.wait_ge(SM["vCh"].h, SM["vCh"].n)
                        u_norm.append(SM["vMi"].inc(v.tensor_scalar(
                            pnb(u)[:, 0:L], psb(u)[:, 0:L],
                            stat2[:, 2 + u % 2:3 + u % 2], None, OP.mult)))

                    emit_tern(B, u * n_fac_gu // NUNITS,
                              (u + 1) * n_fac_gu // NUNITS)
            emit_ptav(NUNITS - 1)

        with nc.Block() as B:
            @B.gpsimd
            def _(g):
                g.wait_ge(SP["tf"][0].h, SP["tf"][0].n)
                g.wait_ge(SP["tf"][1].h, SP["tf"][1].n)
                g.wait_ge(SP["tf"][2].h, SP["tf"][2].n)
                g.wait_ge(SP["tf"][3].h, SP["tf"][3].n)
                g.wait_ge(SP["tfw"][0].h, SP["tfw"][0].n)
                g.wait_ge(SP["tfw"][1].h, SP["tfw"][1].n)

        p3p.close()
        p3.close()
        dump("attf", lambda: qf[:, :])
        dump("wsc", lambda: wsc[:, :])
    p13.close()
    aux_st.close()

    if upto >= 4:
        p45 = ExitStack()
        xqTf = p45.enter_context(nc.sbuf_tensor("xqTf", [128, SLOTS * D], BF16, side="right"))
        p4 = ExitStack()
        x1sl = _dbuf(p4, nc, "x1sl", 512, F32)
        x1full = _dbuf(p4, nc, "x1full", D, F32)
        wT4 = _dbuf(p4, nc, "wT4", DC * 512, BF16)
        xq32 = _dbuf(p4, nc, "xq32", 512, F32)
        x1n = _dbuf(p4, nc, "x1n", D, F32)
        q32a = p4.enter_context(nc.sbuf_tensor("q32a", [128, D], F32))
        q32b = p4.enter_context(nc.sbuf_tensor("q32b", [128, D], F32))
        q32c = p4.enter_context(nc.sbuf_tensor("q32c", [128, D], F32))
        q32i = p4.enter_context(nc.sbuf_tensor("q32i", [128, D], I32))
        sgnb = p4.enter_context(nc.sbuf_tensor("sgnb", [128, D], BF16))
        xqbt = p4.enter_context(nc.sbuf_tensor("xqbt", [128, D], BF16))
        p4p = ExitStack()
        tpa4 = p4p.enter_context(nc.psum_tensor("tpa4", [128, 1024], BF16))
        tpb4 = p4p.enter_context(nc.psum_tensor("tpb4", [128, 1024], BF16))
        ops4 = p4p.enter_context(nc.psum_tensor("ops4", [128, 1024], F32))

        with nc.Block() as B:
            o_mm, o_ev, xq_dma, x1_wr = [], [], [], []
            wt4_marks = []
            for oc in range(NOC):
                @B.scalar
                def _(s, oc=oc):
                    if oc == 0:
                        s.wait_ge(SM["cWO"].h, SM["cWO"].n)
                    if oc >= 2:
                        s.wait_ge(SM["tMM"].h, o_mm[-1])
                    for j in range(DC):
                        SP["wst4"][oc % 2].inc(s.dma_start(
                            wT4(oc)[:, j * 512:(j + 1) * 512],
                            wo_bf.ap()[oc * 512:(oc + 1) * 512,
                                       j * 128:(j + 1) * 128],
                            transpose=True), 16)
                wt4_marks.append(SP["wst4"][oc % 2].n)

                for i in range(SLOTS):
                    n_o = len(o_mm)

                    @B.sync
                    def _(s, oc=oc, i=i, n_o=n_o):
                        if n_o >= 2:
                            s.wait_ge(SM["vMi"].h, o_ev[n_o - 2])
                        xq_dma.append(SP["xq32"][n_o % 2].inc(s.dma_start(
                            xq32(n_o),
                            x_q.ap()[i * 128:(i + 1) * 128, oc * 512:(oc + 1) * 512]),
                            16))

                    @B.tensor
                    def _(te, oc=oc, i=i, n_o=n_o):
                        if i == 0:
                            te.wait_ge(SP["wst4"][oc % 2].h, wt4_marks[oc])
                        if n_o >= 2:
                            te.wait_ge(SM["vMi"].h, o_ev[n_o - 2])
                        sl = ops4[:, (n_o % 2) * 512:(n_o % 2) * 512 + 512]
                        for j in range(DC):
                            te.matmul(sl,
                                      qf[:, (i * NH + j) * 128:(i * NH + j + 1) * 128],
                                      wT4(oc)[:, j * 512:(j + 1) * 512],
                                      start=(j == 0), stop=False)
                        o_mm.append(SM["tMM"].inc(te.matmul(
                            sl, ones_bf[0:1, :],
                            bias_bf[0:1, D + oc * 512: D + (oc + 1) * 512],
                            start=False, stop=True)))

                    @B.vector
                    def _(v, oc=oc, i=i, n_o=n_o):
                        v.wait_ge(SM["tMM"].h, o_mm[n_o])
                        v.wait_ge(SP["xq32"][n_o % 2].h, xq_dma[n_o])
                        if n_o >= 2:
                            v.wait_ge(SP["x1rw"][n_o % 2].h, x1_wr[n_o - 2])
                        sl = ops4[:, (n_o % 2) * 512:(n_o % 2) * 512 + 512]
                        o_ev.append(SM["vMi"].inc(v.scalar_tensor_tensor(
                            x1sl(n_o), sl, 1.0, xq32(n_o), OP.mult, OP.add,
                            accum_out=stat[:, i * NOC + oc: i * NOC + oc + 1])))

                    @B.sync
                    def _(s, oc=oc, i=i, n_o=n_o):
                        s.wait_ge(SM["vMi"].h, o_ev[n_o])
                        x1_wr.append(SP["x1rw"][n_o % 2].inc(s.dma_start(
                            x1raw.ap()[i * 128:(i + 1) * 128,
                                       oc * 512:(oc + 1) * 512], x1sl(n_o)), 16))

        sl_spill = []
        sl_sign = []
        sl_tq = []
        sl_tmark = []
        sl_sq = []
        sl_qd = []
        x1w_tot = (SP["x1rw"][0].n, SP["x1rw"][1].n)
        x1_rdm = []
        with nc.Block() as B:
            for i in range(SLOTS):
                @B.scalar
                def _(sc, i=i):
                    if i == 0:
                        sc.wait_ge(SP["x1rw"][0].h, x1w_tot[0])
                        sc.wait_ge(SP["x1rw"][1].h, x1w_tot[1])
                    if i >= 2:
                        sc.wait_ge(SM["sEx"].h, sl_sq[i - 2])
                        sc.wait_ge(SM["vMi"].h, sl_qd[i - 2])
                    x1_rdm.append(SP["x1fr"][i % 2].inc(sc.dma_start(
                        x1full(i), x1raw.ap()[i * 128:(i + 1) * 128, :]), 16))

                @B.vector
                def _(v, i=i):
                    v.wait_ge(SP["x1fr"][i % 2].h, x1_rdm[i])
                    SM["vCh"].inc(v.reduce_sum(stat2[:, 16:17],
                                               stat[:, i * NOC:i * NOC + NOC], AX.X))
                    v.wait_ge(SM["vCh"].h, SM["vCh"].n)
                    SM["vCh"].inc(v.tensor_scalar(stat2[:, 17:18], stat2[:, 16:17],
                                                  1.0 / D, None, OP.mult))
                    v.wait_ge(SM["vCh"].h, SM["vCh"].n)
                    SM["vMi"].inc(v.engine_nop() if False else v.tensor_copy(
                        stat2[:, 27:28], stat2[:, 17:18]))
                m_done = SM["vMi"].n

                @B.scalar
                def _(sc, i=i, m_done=m_done):
                    sc.wait_ge(SM["vMi"].h, m_done)
                    sc.wait_ge(SP["x1fr"][i % 2].h, x1_rdm[i])
                    sl_sq.append(SM["sEx"].inc(sc.activation(
                        junk_bf[:, 0:D], x1full(i),
                        AF.Square, bias=0.0, scale=1.0,
                        accum_out=stat2[:, 18:19])))
                sq_done = SM["sEx"].n

                @B.vector
                def _(v, i=i, sq_done=sq_done):
                    v.wait_ge(SM["sEx"].h, sq_done)
                    v.tensor_scalar(stat2[:, 19:20], stat2[:, 18:19],
                                    1.0 / D, None, OP.mult)
                    SM["vCh"].inc(v.tensor_tensor(stat2[:, 20:21], stat2[:, 17:18],
                                                  stat2[:, 17:18], OP.mult))
                    v.wait_ge(SM["vCh"].h, SM["vCh"].n)
                    SM["vCh"].inc(v.tensor_tensor(stat2[:, 21:22], stat2[:, 19:20],
                                                  stat2[:, 20:21], OP.subtract))
                    v.wait_ge(SM["vCh"].h, SM["vCh"].n)
                    SM["vMi"].inc(v.tensor_scalar(stat2[:, 22:23], stat2[:, 21:22],
                                                  LN_EPS, None, OP.add))
                var_done = SM["vMi"].n

                @B.scalar
                def _(sc, var_done=var_done):
                    sc.wait_ge(SM["vMi"].h, var_done)
                    SM["sEx"].inc(sc.activation(stat2[:, 23:24], stat2[:, 22:23],
                                                AF.Sqrt, bias=0.0, scale=1.0))
                sd_done = SM["sEx"].n

                @B.vector
                def _(v, i=i, sd_done=sd_done):
                    v.wait_ge(SM["sEx"].h, sd_done)
                    if i >= 2:
                        v.wait_ge(SM["sEx"].h, sl_sign[i - 2])
                        v.wait_ge(SP["x1sp"][i % 2].h, sl_spill[i - 2])
                    SM["vCh"].inc(v.reciprocal(stat2[:, 24:25], stat2[:, 23:24]))
                    v.wait_ge(SM["vCh"].h, SM["vCh"].n)
                    SM["vCh"].inc(v.tensor_scalar(stat2[:, 25:26], stat2[:, 17:18],
                                                  -1.0, None, OP.mult))
                    v.wait_ge(SM["vCh"].h, SM["vCh"].n)
                    SM["vCh"].inc(v.tensor_scalar(x1n(i), x1full(i),
                                                  stat2[:, 25:26], stat2[:, 24:25],
                                                  OP.add, OP.mult))
                    v.wait_ge(SM["vCh"].h, SM["vCh"].n)
                    SM["vCh"].inc(v.reduce_max(stat2[:, 26:27], x1n(i), AX.X,
                                               apply_absolute_value=True))
                    v.wait_ge(SM["vCh"].h, SM["vCh"].n)
                    SM["vCh"].inc(v.tensor_scalar(scl["s_x"][:, i:i + 1],
                                                  stat2[:, 26:27], QEPS, 1.0 / 127.0,
                                                  OP.max, OP.mult))
                    v.wait_ge(SM["vCh"].h, SM["vCh"].n)
                    SM["vCh"].inc(v.reciprocal(scl["r_x"][:, i:i + 1],
                                               scl["s_x"][:, i:i + 1]))
                    v.wait_ge(SM["vCh"].h, SM["vCh"].n)
                    if i >= 1:
                        v.wait_ge(SM["tXT"].h, sl_tmark[i - 1])
                    SM["vMi"].inc(v.nop())
                a_rdy = SM["vMi"].n

                @B.scalar
                def _(sc, i=i, a_rdy=a_rdy):
                    sc.wait_ge(SM["vMi"].h, a_rdy)
                    sc.activation(sgnb[:, :], x1n(i), AF.Sign, bias=0.0, scale=1.0)
                    sl_sign.append(SM["sEx"].inc(sc.activation(
                        q32a[:, :], x1n(i), AF.Abs, bias=0.0,
                        scale=scl["r_x"][:, i:i + 1])))
                abs_done = SM["sEx"].n

                @B.vector
                def _(v, i=i, abs_done=abs_done):
                    v.wait_ge(SM["sEx"].h, abs_done)
                    SM["vCh"].inc(v.tensor_scalar(q32b[:, :], q32a[:, :], 0.5,
                                                  None, OP.add))
                    v.wait_ge(SM["vCh"].h, SM["vCh"].n)
                    SM["vCh"].inc(v.tensor_copy(q32i[:, :], q32b[:, :]))
                    v.wait_ge(SM["vCh"].h, SM["vCh"].n)
                    SM["vCh"].inc(v.tensor_copy(q32c[:, :], q32i[:, :]))
                    v.wait_ge(SM["vCh"].h, SM["vCh"].n)
                    SM["vCh"].inc(v.tensor_tensor(q32a[:, :], q32c[:, :], q32b[:, :],
                                                  OP.is_gt))
                    v.wait_ge(SM["vCh"].h, SM["vCh"].n)
                    SM["vMi"].inc(v.tensor_tensor(q32b[:, :], q32c[:, :], q32a[:, :],
                                                  OP.subtract))
                q_done = SM["vMi"].n
                sl_qd.append(q_done)

                @B.sync
                def _(s, i=i, q_done=q_done):
                    s.wait_ge(SM["vMi"].h, q_done)
                    sl_spill.append(SP["x1sp"][i % 2].inc(s.dma_start(
                        x1scr.ap()[i * 128:(i + 1) * 128, :], x1n(i)), 16))

                @B.vector
                def _(v, i=i, q_done=q_done):
                    v.wait_ge(SM["sEx"].h, sl_sign[i])
                    v.wait_ge(SM["vMi"].h, q_done)
                    sl_tq.append(SM["vMi"].inc(v.tensor_tensor(
                        xqbt[:, :], q32b[:, :], sgnb[:, :], OP.mult)))

                @B.tensor
                def _(te, i=i):
                    te.wait_ge(SM["vMi"].h, sl_tq[i])
                    for j in range(DC):
                        bank = (tpa4, tpb4)[(j // 8) % 2]
                        if j % 8 == 0:
                            te.wait_ge(SM["sEv"].h, bank_free.get(id(bank), 0))
                        ins = te.transpose(bank[:, (j % 8) * 128:(j % 8 + 1) * 128],
                                           xqbt[:, j * 128:(j + 1) * 128], ident[:, :])
                        if j % 8 == 7 or j == DC - 1:
                            SM["tXT"].inc(ins)
                sl_tmark.append(SM["tXT"].n)
                ngrp = (DC + 7) // 8
                t_mark = SM["tXT"].n

                @B.vector
                def _(v, i=i, t_mark=t_mark, ngrp=ngrp):
                    for gi in range(ngrp):
                        bank = (tpa4, tpb4)[gi % 2]
                        v.wait_ge(SM["tXT"].h, t_mark - ngrp + 1 + gi)
                        lo = gi * 8
                        hi = min(lo + 8, DC)
                        dst = xqTf[:].rearrange("p (j k) -> p j k", j=DC) \
                            [:, lo:hi, i * 128:(i + 1) * 128]
                        SM["sEv"].inc(v.tensor_copy(dst, bank[:, 0:(hi - lo) * 128]))
                        bank_free[id(bank)] = SM["sEv"].n
        with nc.Block() as B:
            @B.sync
            def _(s):
                s.wait_ge(SP["x1sp"][0].h, SP["x1sp"][0].n)
                s.wait_ge(SP["x1sp"][1].h, SP["x1sp"][1].n)
        p4p.close()
        p4.close()
        dump("xqTf", lambda: xqTf[:, :])
        dump("sx", lambda: scl["s_x"][:, :])

    attn_st.close()
    if upto >= 5:
        p5 = ExitStack()
        fac_alloc(p5, "b", swdge=True)
        sg2 = p5.enter_context(nc.sbuf_tensor("sg2", [128, 512], F32))
        wTg = _dbuf(p5, nc, "wTg", DC * 512, BF16, n=3)
        wTu = _dbuf(p5, nc, "wTu", DC * 512, BF16, n=3)
        sgb = p5.enter_context(nc.sbuf_tensor("sgb", [128, 512], F32))
        hb5 = _dbuf(p5, nc, "hb5", 512, F32)
        p5p = ExitStack()
        gps = p5p.enter_context(nc.psum_tensor("gps", [128, 2048], F32))
        ups = p5p.enter_context(nc.psum_tensor("ups", [128, 2048], F32))

        with nc.Block() as B:
            @B.vector
            def _(v):
                v.tensor_scalar(scl["sgx"][:, :], scl["s_x"][:, :],
                                wsc[:, 0:1], None, OP.mult)
                SM["vCh"].inc(v.tensor_scalar(scl["sux"][:, :], scl["s_x"][:, :],
                                              wsc[:, 1:2], None, OP.mult))
                SM["vMi"].inc(v.nop())
                v.wait_ge(SM["vCh"].h, SM["vCh"].n)

        h_wr = {}
        sg_done = {}
        n_wd_fac = n_fac - n_fac_gu
        with nc.Block() as B:
            gu_mm, gu_ev, chunk_mm, w_ready = [], [], [], {}
            pfree = []
            tpc = 512 // 128
            nwu = DFF // 128

            def emit_wload(c):
                @B.sync
                def _(s, c=c):
                    th = (c + 1) * tpc - 1
                    s.wait_ge(SP["tfw"][th % 2].h, fac_wb[th])
                    s.wait_ge(SP["tfw"][(th - 1) % 2].h, fac_wb[th - 1])
                    if c >= 3:
                        s.wait_ge(SM["tMM"].h, chunk_mm[c - 3])
                    for j in range(DC):
                        SP["w5st"][c % 3].inc(s.dma_start(
                            wTg(c)[:, j * 512:(j + 1) * 512],
                            wg_q.ap()[c * 512:(c + 1) * 512,
                                      j * 128:(j + 1) * 128],
                            transpose=True), 16)

                @B.sync
                def _(s, c=c):
                    th = nwu + (c + 1) * tpc - 1
                    s.wait_ge(SP["tfw"][th % 2].h, fac_wb[th])
                    s.wait_ge(SP["tfw"][(th - 1) % 2].h, fac_wb[th - 1])
                    if c >= 3:
                        s.wait_ge(SM["tMM"].h, chunk_mm[c - 3])
                    for j in range(DC):
                        SP["w5st"][3 + c % 3].inc(s.dma_start(
                            wTu(c)[:, j * 512:(j + 1) * 512],
                            wu_q.ap()[c * 512:(c + 1) * 512,
                                      j * 128:(j + 1) * 128],
                            transpose=True), 16)
                w_ready[c] = (SP["w5st"][c % 3].n, SP["w5st"][3 + c % 3].n)

            for c0 in range(min(3, FFC)):
                emit_wload(c0)
            for c in range(FFC):
                for tt in range(SLOTS):
                    n_g = len(gu_mm)

                    @B.tensor
                    def _(te, c=c, tt=tt, n_g=n_g):
                        if tt == 0:
                            te.wait_ge(SP["w5st"][c % 3].h, w_ready[c][0])
                            te.wait_ge(SP["w5st"][3 + c % 3].h, w_ready[c][1])
                        if n_g >= 4:
                            te.wait_ge(SM["vMi"].h, pfree[n_g - 4])
                        qo = (n_g % 4) * 512
                        last = None
                        for j in range(DC):
                            lhs = xqTf[:, j * NTOK + tt * 128:
                                       j * NTOK + (tt + 1) * 128]
                            te.matmul(gps[:, qo:qo + 512], lhs,
                                      wTg(c)[:, j * 512:(j + 1) * 512],
                                      start=(j == 0), stop=(j == DC - 1))
                            last = te.matmul(ups[:, qo:qo + 512], lhs,
                                             wTu(c)[:, j * 512:(j + 1) * 512],
                                             start=(j == 0), stop=(j == DC - 1))
                        gu_mm.append(SM["tMM"].inc(last))

                    @B.vector
                    def _(v, tt=tt, n_g=n_g):
                        v.wait_ge(SM["tMM"].h, gu_mm[n_g])
                        if n_g >= 1:
                            v.wait_ge(SM["sEx"].h, sg_done[n_g - 1])
                        qo = (n_g % 4) * 512
                        SM["vCh"].inc(v.tensor_scalar(sgb[:, :],
                                                      gps[:, qo:qo + 512],
                                                      scl["sgx"][:, tt:tt + 1],
                                                      None, OP.mult))
                    evg_mark = SM["vCh"].n

                    @B.scalar
                    def _(sc, n_g=n_g, evg_mark=evg_mark):
                        sc.wait_ge(SM["vCh"].h, evg_mark)
                        if n_g >= 1:
                            sc.wait_ge(SM["vMi"].h, pfree[n_g - 1])
                        sg_done[n_g] = SM["sEx"].inc(sc.activation(
                            sg2[:, :], sgb[:, :], AF.Sigmoid, bias=0.0, scale=1.0))

                    idx = c * SLOTS + tt
                    n_it = FFC * SLOTS
                    emit_tern(B, n_fac_gu + idx * n_wd_fac // n_it,
                              n_fac_gu + (idx + 1) * n_wd_fac // n_it)

                    @B.vector
                    def _(v, c=c, tt=tt, n_g=n_g):
                        qo = (n_g % 4) * 512
                        if n_g >= 2:
                            v.wait_ge(SP["hwr"][n_g % 2].h, h_wr.get(n_g - 2, 0))
                        v.wait_ge(SM["sEx"].h, sg_done[n_g])
                        pfree.append(SM["vMi"].inc(v.scalar_tensor_tensor(
                            hb5(n_g), ups[:, qo:qo + 512],
                            scl["sux"][:, tt:tt + 1], sg2[:, :],
                            OP.mult, OP.mult)))
                        v.wait_ge(SM["vMi"].h, pfree[-1])
                        SM["vCh"].inc(v.reduce_max(stat2[:, 30:31], hb5(n_g),
                                                   AX.X, apply_absolute_value=True))
                        v.wait_ge(SM["vCh"].h, SM["vCh"].n)
                        gu_ev.append(SM["vMi"].inc(v.tensor_tensor(
                            stat[:, 32 + tt:33 + tt], stat[:, 32 + tt:33 + tt],
                            stat2[:, 30:31], OP.max)))

                    @B.gpsimd
                    def _(g, c=c, tt=tt, n_g=n_g):
                        g.wait_ge(SM["vMi"].h, pfree[n_g])
                        h_wr[n_g] = SP["hwr"][n_g % 2].inc(g.dma_start(
                            hscr.ap()[tt * 128:(tt + 1) * 128, c * 512:(c + 1) * 512],
                            hb5(n_g)), 16)
                chunk_mm.append(SM["tMM"].n)
                if c + 3 < FFC:
                    emit_wload(c + 3)

        with nc.Block() as B:
            @B.vector
            def _(v):
                SM["vCh"].inc(v.tensor_scalar(scl["s_h"][:, :], stat[:, 32:32 + SLOTS],
                                              QEPS, 1.0 / 127.0, OP.max, OP.mult))
                v.wait_ge(SM["vCh"].h, SM["vCh"].n)
                SM["vCh"].inc(v.reciprocal(scl["r_h"][:, :], scl["s_h"][:, :]))
                v.wait_ge(SM["vCh"].h, SM["vCh"].n)
                SM["vCh"].inc(v.tensor_scalar(scl["shd"][:, :], scl["s_h"][:, :],
                                              wsc[:, 2:3], None, OP.mult))
                v.wait_ge(SM["vCh"].h, SM["vCh"].n)
        with nc.Block() as B:
            @B.scalar
            def _(sc):
                sc.wait_ge(SP["hwr"][0].h, SP["hwr"][0].n)
                sc.wait_ge(SP["hwr"][1].h, SP["hwr"][1].n)

            @B.gpsimd
            def _(g):
                g.wait_ge(SP["tf"][0].h, SP["tf"][0].n)
                g.wait_ge(SP["tf"][1].h, SP["tf"][1].n)
                g.wait_ge(SP["tf"][2].h, SP["tf"][2].n)
                g.wait_ge(SP["tf"][3].h, SP["tf"][3].n)
                g.wait_ge(SP["tfg"][0].h, SP["tfg"][0].n)
                g.wait_ge(SP["tfg"][1].h, SP["tfg"][1].n)
                g.wait_ge(SP["tfg"][2].h, SP["tfg"][2].n)
                g.wait_ge(SP["tfg"][3].h, SP["tfg"][3].n)
                g.wait_ge(SP["tfw"][0].h, SP["tfw"][0].n)
                g.wait_ge(SP["tfw"][1].h, SP["tfw"][1].n)
        p5p.close()
        p5.close()
        p45.close()
        dump("sh", lambda: scl["s_h"][:, :])

    if upto >= 6:
        p67 = ExitStack()
        acc6 = p67.enter_context(nc.sbuf_tensor("acc6", [128, SLOTS * D], F32))
        p6 = ExitStack()
        hqT = _dbuf(p6, nc, "hqT", 8 * NTOK, BF16)
        wdT = _dbuf(p6, nc, "wdT", 8 * D, BF16)
        hst6 = _dbuf(p6, nc, "hst6", 1024, F32)
        h32a = p6.enter_context(nc.sbuf_tensor("h32a", [128, 1024], F32, side="right"))
        h32b = p6.enter_context(nc.sbuf_tensor("h32b", [128, 1024], F32, side="right"))
        h32c = p6.enter_context(nc.sbuf_tensor("h32c", [128, 1024], F32, side="right"))
        h32i = p6.enter_context(nc.sbuf_tensor("h32i", [128, 1024], I32, side="right"))
        sg6 = p6.enter_context(nc.sbuf_tensor("sg6", [128, 1024], BF16, side="right"))
        hqb = p6.enter_context(nc.sbuf_tensor("hqb", [128, 1024], BF16))
        p6p = ExitStack()
        tpa6 = p6p.enter_context(nc.psum_tensor("tpa6", [128, 1024], BF16))
        tpb6 = p6p.enter_context(nc.psum_tensor("tpb6", [128, 1024], BF16))
        dn6 = p6p.enter_context(nc.psum_tensor("dn6", [128, 2048], F32))

        hwr_tot = (SP["hwr"][0].n, SP["hwr"][1].n)
        with nc.Block() as B:
            grp_mm_last = []
            hq_ready, hq_ts, hq_cp = {}, {}, {}

            def emit_hq(gidx, tt):
                n_h = gidx * SLOTS + tt
                ffb = gidx * 1024

                @B.scalar
                def _(sc, gidx=gidx, tt=tt, n_h=n_h, ffb=ffb):
                    if n_h == 0:
                        sc.wait_ge(SP["hwr"][0].h, hwr_tot[0])
                        sc.wait_ge(SP["hwr"][1].h, hwr_tot[1])
                    if n_h >= 2:
                        sc.wait_ge(SM["vMi"].h, hq_q.get(n_h - 2, 0))
                        sc.wait_ge(SM["sEx"].h, hq_sg.get(n_h - 2, 0))
                    hq_dma[n_h] = SP["hst6"][n_h % 2].inc(sc.dma_start(
                        hst6(n_h),
                        hscr.ap()[tt * 128:(tt + 1) * 128, ffb:ffb + 1024]), 16)

                @B.scalar
                def _(sc, tt=tt, n_h=n_h):
                    sc.wait_ge(SP["hst6"][n_h % 2].h, hq_dma[n_h])
                    if n_h >= 1:
                        sc.wait_ge(SM["vMi"].h, hq_bf.get(n_h - 1, 0))
                    sc.activation(sg6[:, :], hst6(n_h), AF.Sign,
                                  bias=0.0, scale=1.0)
                    hq_sg[n_h] = SM["sEx"].inc(sc.activation(
                        h32a[:, :], hst6(n_h), AF.Abs, bias=0.0,
                        scale=scl["r_h"][:, tt:tt + 1]))

                @B.vector
                def _(v, n_h=n_h):
                    v.wait_ge(SM["sEx"].h, hq_sg[n_h])
                    if n_h >= 1:
                        v.wait_ge(SM["sEx"].h, hq_cp.get(n_h - 1, 0))
                    hq_ts[n_h] = SM["vCh"].inc(v.tensor_scalar(
                        h32b[:, :], h32a[:, :], 0.5, None, OP.add))

                @B.scalar
                def _(sc, n_h=n_h):
                    sc.wait_ge(SM["vCh"].h, hq_ts[n_h])
                    if n_h >= 1:
                        sc.wait_ge(SM["vMi"].h, hq_q.get(n_h - 1, 0))
                    SM["sEx"].inc(sc.copy(h32i[:, :], h32b[:, :]))
                    sc.wait_ge(SM["sEx"].h, SM["sEx"].n)
                    hq_cp[n_h] = SM["sEx"].inc(sc.copy(h32c[:, :], h32i[:, :]))

                @B.vector
                def _(v, n_h=n_h):
                    v.wait_ge(SM["sEx"].h, hq_cp[n_h])
                    if n_h >= 1:
                        v.wait_ge(SM["tXT"].h, hq_tr_mark.get(n_h - 1, 0))
                    SM["vCh"].inc(v.tensor_tensor(h32a[:, :], h32c[:, :],
                                                  h32b[:, :], OP.is_gt))
                    v.wait_ge(SM["vCh"].h, SM["vCh"].n)
                    hq_q[n_h] = SM["vMi"].inc(v.tensor_tensor(
                        h32b[:, :], h32c[:, :], h32a[:, :], OP.subtract))
                    v.wait_ge(SM["vMi"].h, hq_q[n_h])
                    hq_bf[n_h] = SM["vMi"].inc(v.tensor_tensor(
                        hqb[:, :], h32b[:, :], sg6[:, :], OP.mult))

                @B.tensor
                def _(te, gidx=gidx, tt=tt, n_h=n_h):
                    te.wait_ge(SM["vMi"].h, hq_bf[n_h])
                    if gidx >= 2 and tt == 0:
                        te.wait_ge(SM["tMM"].h, grp_mm_last[gidx - 2])
                    bank = (tpa6, tpb6)[n_h % 2]
                    te.wait_ge(SM["sEv"].h, bank_free.get(id(bank), 0))
                    for j in range(8):
                        ins = te.transpose(bank[:, j * 128:(j + 1) * 128],
                                           hqb[:, j * 128:(j + 1) * 128],
                                           ident[:, :])
                    SM["tXT"].inc(ins)
                    hq_tr_mark[n_h] = SM["tXT"].n
                t_mark = SM["tXT"].n

                @B.vector
                def _(v, gidx=gidx, tt=tt, n_h=n_h, t_mark=t_mark):
                    bank = (tpa6, tpb6)[n_h % 2]
                    v.wait_ge(SM["tXT"].h, t_mark)
                    dst = hqT(gidx).rearrange("p (j k) -> p j k", j=8) \
                        [:, 0:8, tt * 128:(tt + 1) * 128]
                    SM["sEv"].inc(v.tensor_copy(dst, bank[:, 0:1024]))
                    bank_free[id(bank)] = SM["sEv"].n
                if tt == SLOTS - 1:
                    hq_ready[gidx] = SM["sEv"].n

            def emit_wdload(gidx):
                ffb = gidx * 1024

                @B.sync
                def _(s, gidx=gidx, ffb=ffb):
                    if gidx == 0:
                        s.wait_ge(SP["tfw"][(n_fac - 1) % 2].h, fac_wb[n_fac - 1])
                        s.wait_ge(SP["tfw"][(n_fac - 2) % 2].h, fac_wb[n_fac - 2])
                    if gidx >= 2:
                        s.wait_ge(SM["tMM"].h, grp_mm_last[gidx - 2])
                    for cf8 in range(8):
                        SP["wd6"][gidx % 2].inc(s.dma_start(
                            wdT(gidx)[:, cf8 * D:(cf8 + 1) * D],
                            wd_q.ap()[0:D, ffb + cf8 * 128:
                                      ffb + (cf8 + 1) * 128],
                            transpose=True), 16)
                return SP["wd6"][gidx % 2].n

            wd_ready = {0: emit_wdload(0)}
            for tt in range(SLOTS):
                emit_hq(0, tt)
            if NGR > 1:
                wd_ready[1] = emit_wdload(1)

            assert NOC <= 4
            for gidx in range(NGR):
                for tt in range(SLOTS):
                    n_c = gidx * SLOTS + tt

                    @B.tensor
                    def _(te, gidx=gidx, tt=tt, n_c=n_c):
                        te.wait_ge(SM["sEv"].h, hq_ready[gidx])
                        if tt == 0:
                            te.wait_ge(SP["wd6"][gidx % 2].h, wd_ready[gidx])
                        if n_c >= 1:
                            te.wait_ge(SM["vMi"].h, dn_ev.get(n_c - 1, 0))
                        last = None
                        for cf8 in range(8):
                            lhs = hqT(gidx)[:, cf8 * NTOK + tt * 128:
                                            cf8 * NTOK + (tt + 1) * 128]
                            for oc2 in range(NOC):
                                last = te.matmul(
                                    dn6[:, oc2 * 512:(oc2 + 1) * 512], lhs,
                                    wdT(gidx)[:, cf8 * D + oc2 * 512:
                                              cf8 * D + (oc2 + 1) * 512],
                                    start=(cf8 == 0), stop=(cf8 == 7))
                        dn_mm[n_c] = SM["tMM"].inc(last)

                    @B.vector
                    def _(v, gidx=gidx, tt=tt, n_c=n_c):
                        v.wait_ge(SM["tMM"].h, dn_mm[n_c])
                        last = None
                        for oc2 in range(NOC):
                            dst = acc6[:, tt * D + oc2 * 512:
                                       tt * D + (oc2 + 1) * 512]
                            if gidx == 0:
                                last = v.tensor_scalar(
                                    dst, dn6[:, oc2 * 512:(oc2 + 1) * 512],
                                    scl["shd"][:, tt:tt + 1], None, OP.mult)
                            else:
                                last = v.scalar_tensor_tensor(
                                    dst, dn6[:, oc2 * 512:(oc2 + 1) * 512],
                                    scl["shd"][:, tt:tt + 1], dst, OP.mult, OP.add)
                        dn_ev[n_c] = SM["vMi"].inc(last)

                    if gidx + 1 < NGR:
                        emit_hq(gidx + 1, tt)
                grp_mm_last.append(SM["tMM"].n)
                if gidx + 2 < NGR:
                    wd_ready[gidx + 2] = emit_wdload(gidx + 2)

        p6p.close()
        p6.close()
        dump("acc6", lambda: acc6[:, :])
        if upto < 7:
            p67.close()

    if upto >= 7:
        p7 = ExitStack()
        x1n2 = _dbuf(p7, nc, "x1n2", D, F32)
        x2b7 = _dbuf(p7, nc, "x2b7", D, F32)
        outb = _dbuf(p7, nc, "outb", D, F32)
        x1_rd, r2_done, sq7, out_dma = {}, {}, {}, {}
        sp_tot = (SP["x1sp"][0].n, SP["x1sp"][1].n)
        with nc.Block() as B:
            for tt in range(SLOTS):
                @B.scalar
                def _(sc, tt=tt):
                    if tt == 0:
                        sc.wait_ge(SP["x1sp"][0].h, sp_tot[0])
                        sc.wait_ge(SP["x1sp"][1].h, sp_tot[1])
                    if tt >= 2:
                        sc.wait_ge(SM["sEx"].h, sq7[tt - 2])
                    x1_rd[tt] = SP["x1rd"][tt % 2].inc(sc.dma_start(
                        x1n2(tt), x1scr.ap()[tt * 128:(tt + 1) * 128, :]), 16)

                @B.vector
                def _(v, tt=tt):
                    v.wait_ge(SP["x1rd"][tt % 2].h, x1_rd[tt])
                    if tt >= 2:
                        v.wait_ge(SP["outw"][tt % 2].h, out_dma[tt - 2])
                    r2_done[tt] = SM["vMi"].inc(v.scalar_tensor_tensor(
                        x2b7(tt), acc6[:, tt * D:(tt + 1) * D], 1.0,
                        x1n2(tt), OP.mult, OP.add,
                        accum_out=stat2[:, 40:41]))

                @B.scalar
                def _(sc, tt=tt):
                    sc.wait_ge(SM["vMi"].h, r2_done[tt])
                    sq7[tt] = SM["sEx"].inc(sc.activation(
                        junk_bf[:, 0:D], x2b7(tt), AF.Square, bias=0.0,
                        scale=1.0, accum_out=stat2[:, 41:42]))

                @B.vector
                def _(v, tt=tt):
                    v.wait_ge(SM["sEx"].h, sq7[tt])
                    v.tensor_scalar(stat2[:, 42:43], stat2[:, 40:41],
                                    1.0 / D, None, OP.mult)
                    SM["vCh"].inc(v.tensor_scalar(stat2[:, 43:44], stat2[:, 41:42],
                                                  1.0 / D, None, OP.mult))
                    v.wait_ge(SM["vCh"].h, SM["vCh"].n)
                    SM["vCh"].inc(v.tensor_tensor(stat2[:, 44:45], stat2[:, 42:43],
                                                  stat2[:, 42:43], OP.mult))
                    v.wait_ge(SM["vCh"].h, SM["vCh"].n)
                    SM["vCh"].inc(v.tensor_tensor(stat2[:, 45:46], stat2[:, 43:44],
                                                  stat2[:, 44:45], OP.subtract))
                    v.wait_ge(SM["vCh"].h, SM["vCh"].n)
                    SM["vMi"].inc(v.tensor_scalar(stat2[:, 46:47], stat2[:, 45:46],
                                                  LN_EPS, None, OP.add))
                var7 = SM["vMi"].n

                @B.scalar
                def _(sc, var7=var7):
                    sc.wait_ge(SM["vMi"].h, var7)
                    SM["sEx"].inc(sc.activation(stat2[:, 47:48], stat2[:, 46:47],
                                                AF.Sqrt, bias=0.0, scale=1.0))
                sd7 = SM["sEx"].n

                @B.vector
                def _(v, tt=tt, sd7=sd7):
                    v.wait_ge(SM["sEx"].h, sd7)
                    SM["vCh"].inc(v.reciprocal(stat2[:, 48:49], stat2[:, 47:48]))
                    v.wait_ge(SM["vCh"].h, SM["vCh"].n)
                    SM["vCh"].inc(v.tensor_scalar(stat2[:, 49:50], stat2[:, 42:43],
                                                  -1.0, None, OP.mult))
                    v.wait_ge(SM["vCh"].h, SM["vCh"].n)
                    SM["vMi"].inc(v.tensor_scalar(outb(tt), x2b7(tt),
                                                  stat2[:, 49:50], stat2[:, 48:49],
                                                  OP.add, OP.mult))
                o7 = SM["vMi"].n

                @B.sync
                def _(s, tt=tt, o7=o7):
                    s.wait_ge(SM["vMi"].h, o7)
                    out_dma[tt] = SP["outw"][tt % 2].inc(s.dma_start(
                        out_d.ap()[tt * 128:(tt + 1) * 128, :], outb(tt)), 16)
        p7.close()
        p67.close()

    with nc.Block() as B:
        @B.sync
        def _(s):
            s.wait_ge(SP["outw"][0].h, SP["outw"][0].n)
            s.wait_ge(SP["outw"][1].h, SP["outw"][1].n)
            s.wait_ge(SM["dDbg"].h, SM["dDbg"].n)
    st.close()
    return nc


def _prep_inputs(cfg, inputs, core):
    D, SLOTS, NB = cfg["D"], cfg["SLOTS"], cfg["NB"]
    b, k = core // 4, core % 4
    x = np.asarray(inputs["L0_attn_x"], dtype=np.float32)
    blocks = [4 * i + k for i in range(SLOTS)]
    xb = x[b]
    x_q = np.concatenate([xb[g * 128:(g + 1) * 128] for g in blocks], axis=0)
    qpos = np.empty((128, SLOTS), np.float32)
    for i, g in enumerate(blocks):
        qpos[:, i] = g * 128 + np.arange(128)
    thr = (128.0 * k + np.arange(128) + 1.0).astype(np.float32).reshape(128, 1)
    m = {
        "x_q": np.ascontiguousarray(x_q),
        "x_kv": np.ascontiguousarray(xb),
        "wq": np.asarray(inputs["L0_attn_Wq"], np.float32),
        "bq": np.asarray(inputs["L0_attn_bq"], np.float32).reshape(1, -1),
        "wk": np.asarray(inputs["L0_attn_Wk"], np.float32),
        "bk": np.asarray(inputs["L0_attn_bk"], np.float32).reshape(1, -1),
        "wv": np.asarray(inputs["L0_attn_Wv"], np.float32),
        "bv": np.asarray(inputs["L0_attn_bv"], np.float32).reshape(1, -1),
        "wo": np.asarray(inputs["L0_attn_Wo"], np.float32),
        "bo": np.asarray(inputs["L0_attn_bo"], np.float32).reshape(1, -1),
        "wg": np.asarray(inputs["L0_ffn_w_gate"], np.float32),
        "wu": np.asarray(inputs["L0_ffn_w_up"], np.float32),
        "wd": np.asarray(inputs["L0_ffn_w_down"], np.float32),
        "qpos": qpos,
        "thr": thr,
    }
    return m


def kernel(**inputs):
    cfg = real_cfg()
    D, SLOTS = cfg["D"], cfg["SLOTS"]
    nc = build_nc(cfg)
    in_maps = [_prep_inputs(cfg, inputs, c) for c in range(8)]
    res = run_bass_kernel_spmd(nc, in_maps, core_ids=list(range(8)))
    x = np.asarray(inputs["L0_attn_x"], dtype=np.float32)
    out = np.empty_like(x)
    for c in range(8):
        b, k = c // 4, c % 4
        oc = np.asarray(res.results[c]["out"])
        for i in range(SLOTS):
            g = 4 * i + k
            out[b, g * 128:(g + 1) * 128] = oc[i * 128:(i + 1) * 128]
    return out
